# version 02
# baseline (speedup 1.0000x reference)
import contextlib
import numpy as np
import ml_dtypes
import bass_rust
import concourse.bass as bass
import concourse.mybir as mybir
from concourse.bass_utils import run_bass_kernel_spmd

F32 = mybir.dt.float32
BF16 = mybir.dt.bfloat16
I32 = mybir.dt.int32
AF = mybir.ActivationFunctionType
ALU = mybir.AluOpType
AX = mybir.AxisListType

B, S, D = 4, 8192, 1024
H = 16
HID = 3072
KT = D // 128
FT = HID // 128
NT = S // 128
SCALE_W = 1.0 / 127.0
EPS = 1e-5


def ap_nd(base_ap, extra_offset, dims):
    ap = base_ap
    ap.ap = bass_rust.VecI64Pair(dims)
    ap.offset = ap.offset + extra_offset
    return ap


def build():
    nc = bass.Bass("TRN2", target_bir_lowering=False)

    xt = nc.dram_tensor("xt", [D, S], F32, kind="ExternalInput").ap()
    wg = nc.dram_tensor("wg", [HID, D], I32, kind="ExternalInput").ap()
    wu = nc.dram_tensor("wu", [HID, D], I32, kind="ExternalInput").ap()
    wd = nc.dram_tensor("wd", [D, HID], I32, kind="ExternalInput").ap()
    wv = nc.dram_tensor("wv", [D, D], F32, kind="ExternalInput").ap()
    wo = nc.dram_tensor("wo", [D, D], F32, kind="ExternalInput").ap()
    wa = nc.dram_tensor("wa", [D, H], F32, kind="ExternalInput").ap()
    bo = nc.dram_tensor("bo", [128, KT], F32, kind="ExternalInput").ap()
    ones = nc.dram_tensor("ones", [128, 1], BF16, kind="ExternalInput").ap()
    ident = nc.dram_tensor("ident", [128, 128], BF16, kind="ExternalInput").ap()
    cstin = nc.dram_tensor("cstin", [128, 2], F32, kind="ExternalInput").ap()
    out = nc.dram_tensor("out", [D, S // 2], F32, kind="ExternalOutput").ap()

    wsg = nc.dram_tensor("wsg", [HID, D], BF16)
    wsu = nc.dram_tensor("wsu", [HID, D], BF16)
    wsd = nc.dram_tensor("wsd", [D, HID], BF16)
    vst = nc.dram_tensor("vst", [S, D], BF16)
    x1st = nc.dram_tensor("x1st", [2, D, S // 2], F32)
    cst = nc.dram_tensor("cst", [2, S // 2, D], BF16)
    zst = nc.dram_tensor("zst", [H, S], F32)

    sched = {"sync": [], "gpsimd": [], "scalar": [], "vector": [], "tensor": []}
    cnt = {}

    def emit(engname, fn):
        sched[engname].append(fn)

    with contextlib.ExitStack() as top:
        E = top.enter_context
        sems = {s: E(nc.semaphore(s)) for s in ["dS", "dG", "cP", "cA", "cV", "cG"]}

        def W(engname, sem_name, val):
            if val <= 0:
                return
            s = sems[sem_name]
            emit(engname, lambda eng, s=s, v=val: eng.wait_ge(s, v))

        def DMA(engname, dst, src, sem_name):
            cnt[sem_name] = cnt.get(sem_name, 0) + 16
            s = sems[sem_name]
            emit(engname, lambda eng, d=dst, so=src, s=s: eng.dma_start(out=d, in_=so).then_inc(s, 16))
            v = cnt[sem_name]
            emit(engname, lambda eng, s=s, v=v: eng.wait_ge(s, v))
            return cnt[sem_name]

        def OP(engname, sem_name, fn):
            cnt[sem_name] = cnt.get(sem_name, 0) + 1
            s = sems[sem_name]
            emit(engname, lambda eng, f=fn, s=s: f(eng).then_inc(s, 1))
            return cnt[sem_name]

        def OPn(engname, fn):
            emit(engname, lambda eng, f=fn: f(eng))

        ident_sb = E(nc.sbuf_tensor("ident_sb", [128, 128], BF16))
        ones_sb = E(nc.sbuf_tensor("ones_sb", [128, 1], BF16))
        csts = E(nc.sbuf_tensor("csts", [128, 2], F32))
        reg_sync = E(nc.sync.register("ridx"))
        reg_holder = {}

        DMA("sync", ident_sb[:, :], ident[:, :], "dS")
        DMA("sync", ones_sb[:, :], ones[:, :], "dS")
        DMA("sync", csts[:, :], cstin[:, :], "dS")

        with contextlib.ExitStack() as p0:
            wis = [p0.enter_context(nc.sbuf_tensor(f"wi{i}", [128, 3072], I32)) for i in range(2)]
            wbs = [p0.enter_context(nc.sbuf_tensor(f"wb{i}", [128, 3072], BF16)) for i in range(2)]
            chunks = []
            for f in range(FT):
                chunks.append((wg, wsg, f * 128, 1024))
                chunks.append((wu, wsu, f * 128, 1024))
            for k in range(KT):
                chunks.append((wd, wsd, k * 128, 3072))
            last_st = {0: 0, 1: 0}
            last_cv = {0: 0, 1: 0}
            for i, (srcd, dstd, r0, ncol) in enumerate(chunks):
                par = i % 2
                W("sync", "cV", last_cv[par])
                ld = DMA("sync", wis[par][:, 0:ncol], srcd[r0:r0 + 128, 0:ncol], "dS")
                W("vector", "dS", ld)
                W("vector", "dG", last_st[par])
                last_cv[par] = OP("vector", "cV", lambda eng, p=par, n=ncol: eng.tensor_scalar(
                    out=wbs[p][:, 0:n], in0=wis[p][:, 0:n],
                    scalar1=SCALE_W, scalar2=None, op0=ALU.mult))
                W("gpsimd", "cV", last_cv[par])
                last_st[par] = DMA("gpsimd", dstd.ap()[r0:r0 + 128, 0:ncol], wbs[par][:, 0:ncol], "dG")
            w_stash_done = cnt["dG"]

        def make_ffn_ctx(Ec, tag):
            ctx = {}
            ctx["xf"] = Ec(nc.sbuf_tensor("xf" + tag, [128, 8192], F32))
            ctx["lnb"] = Ec(nc.sbuf_tensor("lnb" + tag, [128, 8192], BF16))
            ctx["h_sb"] = Ec(nc.sbuf_tensor("h" + tag, [128, FT * 1024], BF16))
            ctx["sg"] = Ec(nc.sbuf_tensor("sg" + tag, [128, 512], BF16))
            ctx["xb"] = Ec(nc.sbuf_tensor("xb" + tag, [128, 512], BF16))
            ctx["sqb"] = Ec(nc.sbuf_tensor("sqb" + tag, [128, 512], BF16))
            ctx["tmpn"] = Ec(nc.sbuf_tensor("tmpn" + tag, [128, 512], F32))
            ctx["tmpg"] = Ec(nc.sbuf_tensor("tmpg" + tag, [128, 512], F32))
            ctx["st"] = Ec(nc.sbuf_tensor("st" + tag, [1, 2560], F32))
            ctx["bcb"] = Ec(nc.sbuf_tensor("bcb" + tag, [128, 1024], F32))
            ctx["wgus"] = [Ec(nc.sbuf_tensor(f"wgu{i}" + tag, [128, 2048], BF16)) for i in range(2)]
            ctx["wdbs"] = [Ec(nc.sbuf_tensor(f"wdb{i}" + tag, [128, 3072], BF16)) for i in range(2)]
            ctx["pA"] = Ec(nc.psum_tensor("pA" + tag, [128, 512], F32))
            ctx["pB"] = Ec(nc.psum_tensor("pB" + tag, [128, 512], F32))
            ctx["pC"] = Ec(nc.psum_tensor("pC" + tag, [128, 512], F32))
            ctx["pD"] = Ec(nc.psum_tensor("pD" + tag, [128, 512], F32))
            ctx["pT"] = [Ec(nc.psum_tensor(f"pT{i}" + tag, [128, 512], BF16)) for i in range(2)]
            return ctx

        def ln_pass(cx, half, pe_clear_v=0, pe_clear_g=0, src_cv=0):
            xf, lnb = cx["xf"], cx["lnb"]
            xb, sqb, st, bcb = cx["xb"], cx["sqb"], cx["st"], cx["bcb"]
            tmpn, tmpg = cx["tmpn"], cx["tmpg"]
            pA, pB = cx["pA"], cx["pB"]
            t0 = half * 512
            if src_cv:
                W("vector", "cV", src_cv)
                W("scalar", "cV", src_cv)
                W("gpsimd", "cV", src_cv)
            for k in range(KT):
                sl = slice(k * 1024 + t0, k * 1024 + t0 + 512)
                cv = OP("vector", "cV", lambda eng, sl=sl: eng.tensor_copy(
                    out=xb[:, :], in_=xf[:, sl]))
                ca = OP("scalar", "cA", lambda eng, sl=sl: eng.activation(
                    out=sqb[:, :], in_=xf[:, sl], func=AF.Square))
                W("tensor", "cV", cv)
                W("tensor", "cA", ca)
                v1 = OP("tensor", "cP", lambda eng, k=k: eng.matmul(
                    pA[0:1, :], ones_sb[:, :], xb[:, :], start=(k == 0), stop=(k == KT - 1)))
                v2 = OP("tensor", "cP", lambda eng, k=k: eng.matmul(
                    pB[0:1, :], ones_sb[:, :], sqb[:, :], start=(k == 0), stop=(k == KT - 1)))
                if k < KT - 1:
                    W("vector", "cP", v1)
                    W("scalar", "cP", v2)
            W("vector", "cP", cnt["cP"])
            W("vector", "cV", cx.get("prev_mr_cv", 0))
            v_ = OP("vector", "cV", lambda eng: eng.tensor_scalar(
                out=st[0:1, 0:512], in0=pA[0:1, :], scalar1=1.0 / D, scalar2=None, op0=ALU.mult))
            pa_free = OP("vector", "cV", lambda eng: eng.tensor_scalar(
                out=st[0:1, 1536:2048], in0=pB[0:1, :], scalar1=1.0 / D, scalar2=None, op0=ALU.mult))
            W("vector", "cV", v_)
            v_ = OP("vector", "cV", lambda eng: eng.tensor_tensor(
                out=st[0:1, 2048:2560], in0=st[0:1, 0:512], in1=st[0:1, 0:512], op=ALU.mult))
            W("vector", "cV", v_)
            cv = OP("vector", "cV", lambda eng: eng.tensor_tensor(
                out=st[0:1, 1536:2048], in0=st[0:1, 1536:2048], in1=st[0:1, 2048:2560], op=ALU.subtract))
            W("scalar", "cV", cv)
            ca = OP("scalar", "cA", lambda eng: eng.activation(
                out=st[0:1, 2048:2560], in_=st[0:1, 1536:2048], func=AF.Sqrt, bias=csts[0:1, 0:1]))
            W("vector", "cA", ca)
            v_ = OP("vector", "cV", lambda eng: eng.reciprocal(
                out=st[0:1, 512:1024], in_=st[0:1, 2048:2560]))
            W("vector", "cV", v_)
            cv = OP("vector", "cV", lambda eng: eng.tensor_tensor(
                out=st[0:1, 1024:1536], in0=st[0:1, 0:512], in1=st[0:1, 512:1024], op=ALU.mult))
            cx["prev_mr_cv"] = cv
            W("gpsimd", "cV", cv)
            W("gpsimd", "cV", cx.get("prev_norm_cv", 0))
            W("gpsimd", "cG", cx.get("prev_norm_cg", 0))
            bsrc = ap_nd(st[0:1, 0:1], 512, [(2560, 1), (0, 128), (1, 1024)])
            bc = DMA("gpsimd", bcb[:, :], bsrc, "dG")
            lastcv = cx.get("prev_norm_cv", 0)
            lastcg = cx.get("prev_norm_cg", 0)
            W("vector", "dG", bc)
            W("gpsimd", "dG", bc)
            if pe_clear_v:
                W("vector", "cP", pe_clear_v)
            if pe_clear_g:
                W("gpsimd", "cP", pe_clear_g)
            for k in range(KT):
                sl = slice(k * 1024 + t0, k * 1024 + t0 + 512)
                if k % 2 == 0:
                    if lastcv:
                        W("vector", "cV", lastcv)
                    v_ = OP("vector", "cV", lambda eng, sl=sl: eng.tensor_tensor(
                        out=tmpn[:, :], in0=xf[:, sl], in1=bcb[:, 0:512], op=ALU.mult))
                    W("vector", "cV", v_)
                    lastcv = OP("vector", "cV", lambda eng, sl=sl: eng.tensor_tensor(
                        out=lnb[:, sl], in0=tmpn[:, :], in1=bcb[:, 512:1024], op=ALU.subtract))
                else:
                    if lastcg:
                        W("gpsimd", "cG", lastcg)
                    v_ = OP("gpsimd", "cG", lambda eng, sl=sl: eng.tensor_tensor(
                        out=tmpg[:, :], in0=xf[:, sl], in1=bcb[:, 0:512], op=ALU.mult))
                    W("gpsimd", "cG", v_)
                    lastcg = OP("gpsimd", "cG", lambda eng, sl=sl: eng.tensor_tensor(
                        out=lnb[:, sl], in0=tmpg[:, :], in1=bcb[:, 512:1024], op=ALU.subtract))
            cx["prev_norm_cv"] = lastcv
            cx["prev_norm_cg"] = lastcg
            return lastcv, lastcg, pa_free

        def ffn_block(cx, wload_gate, half_scale, stash_x1=None, xf_read=(0, 0)):
            lnb, h_sb, sg, xf = cx["lnb"], cx["h_sb"], cx["sg"], cx["xf"]
            wgus, wdbs = cx["wgus"], cx["wdbs"]
            pA, pB, pC, pD = cx["pA"], cx["pB"], cx["pC"], cx["pD"]
            for f in range(FT):
                par = f % 2
                W("sync", "cP", cx.get("wgu_free", {}).get(par, 0))
                DMA("sync", wgus[par][:, 0:1024], wsg.ap()[f * 128:(f + 1) * 128, :], "dS")
                l2 = DMA("sync", wgus[par][:, 1024:2048], wsu.ap()[f * 128:(f + 1) * 128, :], "dS")
                W("tensor", "dS", l2)
                for half in range(2):
                    pg = pA if half == 0 else pC
                    pu = pB if half == 0 else pD
                    t0 = half * 512
                    gate_done = up_done = 0
                    for k in range(KT):
                        mk = (lambda eng, k=k, par=par, pg=pg, t0=t0: eng.matmul(
                            pg[:, :], wgus[par][:, k * 128:(k + 1) * 128],
                            lnb[:, k * 1024 + t0:k * 1024 + t0 + 512],
                            start=(k == 0), stop=(k == KT - 1)))
                        if k == KT - 1:
                            gate_done = OP("tensor", "cP", mk)
                        else:
                            OPn("tensor", mk)
                    for k in range(KT):
                        mk = (lambda eng, k=k, par=par, pu=pu, t0=t0: eng.matmul(
                            pu[:, :], wgus[par][:, 1024 + k * 128:1024 + (k + 1) * 128],
                            lnb[:, k * 1024 + t0:k * 1024 + t0 + 512],
                            start=(k == 0), stop=(k == KT - 1)))
                        if k == KT - 1:
                            up_done = OP("tensor", "cP", mk)
                        else:
                            OPn("tensor", mk)
                    W("scalar", "cP", gate_done)
                    W("scalar", "cV", cx.get("prev_hdone", 0))
                    sgv = OP("scalar", "cA", lambda eng, pg=pg: eng.activation(
                        out=sg[:, :], in_=pg[:, :], func=AF.Silu))
                    W("vector", "cP", up_done)
                    W("vector", "cA", sgv)
                    hdone = OP("vector", "cV", lambda eng, f=f, half=half, pu=pu: eng.tensor_tensor(
                        out=h_sb[:, f * 1024 + half * 512:f * 1024 + half * 512 + 512],
                        in0=sg[:, :], in1=pu[:, :], op=ALU.mult))
                    cx["prev_hdone"] = hdone
                    W("tensor", "cA", sgv)
                    W("tensor", "cV", hdone)
                cx.setdefault("wgu_free", {})[par] = cnt["cP"]
            h_all = cnt["cV"]
            last_dma = 0
            for dm in range(KT):
                par = dm % 2
                W("sync", "cP", cx.get("wdb_free", {}).get(par, 0))
                l = DMA("sync", wdbs[par][:, :], wsd.ap()[dm * 128:(dm + 1) * 128, :], "dS")
                W("tensor", "dS", l)
                W("tensor", "cV", h_all)
                for half in range(2):
                    pd = pA if half == 0 else pC
                    t0 = half * 512
                    dn = 0
                    if dm == 0 and half == 0:
                        W("vector", "cV", xf_read[0])
                        W("vector", "cG", xf_read[1])
                    for f in range(FT):
                        mk = (lambda eng, f=f, par=par, pd=pd, half=half: eng.matmul(
                            pd[:, :], wdbs[par][:, f * 128:(f + 1) * 128],
                            h_sb[:, f * 1024 + half * 512:f * 1024 + half * 512 + 512],
                            start=(f == 0), stop=(f == FT - 1)))
                        if f == FT - 1:
                            dn = OP("tensor", "cP", mk)
                        else:
                            OPn("tensor", mk)
                    W("vector", "cP", dn)
                    x1v = OP("vector", "cV", lambda eng, dm=dm, t0=t0, pd=pd, hs=half_scale: eng.scalar_tensor_tensor(
                        out=xf[:, dm * 1024 + t0:dm * 1024 + t0 + 512],
                        in0=pd[:, :], scalar=hs,
                        in1=xf[:, dm * 1024 + t0:dm * 1024 + t0 + 512],
                        op0=ALU.mult, op1=ALU.add))
                    W("tensor", "cV", x1v)
                cx.setdefault("wdb_free", {})[par] = cnt["cP"]
                if stash_x1 is not None:
                    W("gpsimd", "cV", cnt["cV"])
                    last_dma = DMA("gpsimd", stash_x1(dm), xf[:, dm * 1024:(dm + 1) * 1024], "dG")
            return cnt["cV"], last_dma, cnt["cP"]

        NMACRO = S // 1024
        HFM = max(NMACRO // 2, 1)
        NCH = NT // 8
        with contextlib.ExitStack() as pa_stack:
            Ea = pa_stack.enter_context
            cx = make_ffn_ctx(Ea, "a")
            vf = Ea(nc.sbuf_tensor("vf", [128, 4096], BF16))
            vts = [Ea(nc.sbuf_tensor(f"vt{i}", [128, 1024], BF16)) for i in range(2)]
            wv_sb = Ea(nc.sbuf_tensor("wv_sb", [128, KT * 1024], BF16))
            zstg = Ea(nc.sbuf_tensor("zstg", [H, 512], F32))
            wa_sb = Ea(nc.sbuf_tensor("wa_sb", [128, KT * H], BF16))
            wvt = Ea(nc.sbuf_tensor("wvt", [128, 1024], F32))

            W("sync", "cV", cnt["cV"])
            W("sync", "dG", w_stash_done)
            prev_ca = 0
            for k in range(KT):
                W("sync", "cA", prev_ca)
                ld = DMA("sync", wvt[:, :], wv[k * 128:(k + 1) * 128, :], "dS")
                W("scalar", "dS", ld)
                prev_ca = OP("scalar", "cA", lambda eng, k=k: eng.activation(
                    out=wv_sb[:, k * 1024:(k + 1) * 1024], in_=wvt[:, :], func=AF.Copy))
            wa_src = wa.copy()
            wa_src.ap = bass_rust.VecI64Pair([(H, 128), (128 * H, KT), (1, H)])
            wa_src.offset = 0
            W("sync", "cA", prev_ca)
            wadst = ap_nd(wvt[0:128, 0:1], 0, [(1024, 128), (H, KT), (1, H)])
            ld = DMA("sync", wadst, wa_src, "dS")
            W("scalar", "dS", ld)
            wa_ready = OP("scalar", "cA", lambda eng: eng.activation(
                out=wa_sb[:, :], in_=wvt[:, 0:KT * H], func=AF.Copy))

            vz_pe_done = 0
            lnb_free_pe = 0
            xf_free = (0, 0, 0, 0)
            vt_dma = {0: 0, 1: 0}
            for m in range(NMACRO):
                W("sync", "cV", xf_free[0])
                W("sync", "cG", xf_free[1])
                W("sync", "dG", xf_free[2])
                W("sync", "cA", xf_free[3])
                if m == 0:
                    W("sync", "dG", w_stash_done)
                xld = 0
                for k in range(KT):
                    xld = DMA("sync", cx["xf"][:, k * 1024:(k + 1) * 1024],
                              xt[k * 128:(k + 1) * 128, m * 1024:(m + 1) * 1024], "dS")
                for e in ["vector", "scalar", "tensor", "gpsimd"]:
                    W(e, "dS", xld)
                c0 = ln_pass(cx, 0, pe_clear_v=lnb_free_pe, pe_clear_g=lnb_free_pe)
                c1 = ln_pass(cx, 1)
                W("tensor", "cV", max(c0[0], c1[0]))
                W("tensor", "cG", max(c0[1], c1[1]))
                hf = m // HFM

                def stash(dm, m=m, hf=hf):
                    return x1st.ap()[hf, dm * 128:(dm + 1) * 128,
                                     (m % HFM) * 1024:(m % HFM) * 1024 + 1024]
                lastcv, x1dma, down_pe = ffn_block(cx, None, 0.5, stash_x1=stash,
                                                   xf_read=(max(c0[0], c1[0]), max(c0[1], c1[1])))
                c0 = ln_pass(cx, 0, pe_clear_v=down_pe, pe_clear_g=down_pe, src_cv=lastcv)
                c1 = ln_pass(cx, 1)
                W("tensor", "cV", max(c0[0], c1[0]))
                W("tensor", "cG", max(c0[1], c1[1]))
                W("tensor", "cA", wa_ready)
                lnbA = cx["lnb"]
                pDA = cx["pD"]
                for half in range(2):
                    t0 = half * 512
                    for fv in range(KT):
                        pv = [cx["pA"], cx["pC"], cx["pB"], cx["pD"]][fv % 4]
                        vmm = 0
                        for k in range(KT):
                            mk = (lambda eng, fv=fv, k=k, pv=pv, t0=t0: eng.matmul(
                                pv[:, :], wv_sb[:, k * 1024 + fv * 128:k * 1024 + fv * 128 + 128],
                                lnbA[:, k * 1024 + t0:k * 1024 + t0 + 512],
                                start=(k == 0), stop=(k == KT - 1)))
                            if k == KT - 1:
                                vmm = OP("tensor", "cP", mk)
                            else:
                                OPn("tensor", mk)
                        W("scalar", "cP", vmm)
                        vev = OP("scalar", "cA", lambda eng, fv=fv, pv=pv: eng.activation(
                            out=vf[:, fv * 512:(fv + 1) * 512], in_=pv[:, :], func=AF.Copy))
                        W("tensor", "cA", vev)
                    zmm = 0
                    for k in range(KT):
                        mk = (lambda eng, k=k, t0=t0: eng.matmul(
                            pDA[0:16, :], wa_sb[:, k * H:(k + 1) * H],
                            lnbA[:, k * 1024 + t0:k * 1024 + t0 + 512],
                            start=(k == 0), stop=(k == KT - 1)))
                        if k == KT - 1:
                            zmm = OP("tensor", "cP", mk)
                        else:
                            OPn("tensor", mk)
                    W("scalar", "cP", zmm)
                    W("scalar", "dG", cx.get("zdma", 0))
                    zev = OP("scalar", "cA", lambda eng: eng.activation(
                        out=zstg[:, :], in_=pDA[0:16, :], func=AF.Copy,
                        scale=csts[0:16, 1:2]))
                    W("tensor", "cA", zev)
                    W("gpsimd", "cA", zev)
                    cx["zdma"] = DMA("gpsimd", zst.ap()[0:H, m * 1024 + t0:m * 1024 + t0 + 512],
                                     zstg[:, :], "dG")
                    for s4 in range(4):
                        par = s4 % 2
                        vt = vts[par]
                        pT = cx["pT"][par]
                        t1 = t2 = 0
                        cp1 = 0
                        for fv in range(KT):
                            mk = (lambda eng, fv=fv, s4=s4, pT=pT: eng.transpose(
                                pT[:, (fv % 4) * 128:(fv % 4) * 128 + 128],
                                vf[:, fv * 512 + s4 * 128:fv * 512 + s4 * 128 + 128],
                                ident_sb[:, :]))
                            if fv == 3:
                                t1 = OP("tensor", "cP", mk)
                                W("vector", "cP", t1)
                                W("vector", "dG", vt_dma[par])
                                cp1 = OP("vector", "cV", lambda eng, vt=vt, pT=pT: eng.tensor_copy(
                                    out=vt[:, 0:512], in_=pT[:, :]))
                            elif fv == KT - 1:
                                t2 = OP("tensor", "cP", mk)
                            else:
                                if fv == 4:
                                    W("tensor", "cV", cp1)
                                OPn("tensor", mk)
                        W("vector", "cP", t2)
                        cvv = OP("vector", "cV", lambda eng, vt=vt, pT=pT: eng.tensor_copy(
                            out=vt[:, 512:1024], in_=pT[:, :]))
                        W("gpsimd", "cV", cvv)
                        row0 = (m * 8 + half * 4 + s4) * 128
                        vt_dma[par] = DMA("gpsimd", vst.ap()[row0:row0 + 128, :], vt[:, :], "dG")
                        W("tensor", "cV", cvv)
                vz_pe_done = cnt["cP"]
                lnb_free_pe = vz_pe_done
                xf_free = (cnt["cV"], cnt["cG"], x1dma, cnt["cA"])
            a_end = {k: cnt.get(k, 0) for k in ["cP", "cA", "cV", "cG", "dS", "dG"]}

        zpb_stack = contextlib.ExitStack()
        zpb = zpb_stack.enter_context(nc.sbuf_tensor("zpb", [H, S + 256], BF16))
        with contextlib.ExitStack() as pbs:
            red = pbs.enter_context(nc.sbuf_tensor("red", [H, 2], F32))
            z_sb = pbs.enter_context(nc.sbuf_tensor("z_sb", [H, S], F32))
            W("sync", "dG", a_end["dG"])
            W("sync", "cP", a_end["cP"])
            W("sync", "cA", a_end["cA"])
            W("sync", "cV", a_end["cV"])
            W("sync", "cG", a_end["cG"])
            zld = DMA("sync", z_sb[:, :], zst.ap()[:, :], "dS")
            W("vector", "dS", zld)
            W("vector", "cA", a_end["cA"])
            rv = OP("vector", "cV", lambda eng: eng.reduce_max(
                out=red[:, 0:1], in_=z_sb[:, :], axis=AX.X, negate=True))
            W("scalar", "cV", rv)
            ev = OP("scalar", "cA", lambda eng: eng.activation(
                out=z_sb[:, :], in_=z_sb[:, :], func=AF.Exp, bias=red[:, 0:1]))
            W("vector", "cA", ev)
            v_ = OP("vector", "cV", lambda eng: eng.reduce_sum(
                out=red[:, 1:2], in_=z_sb[:, :], axis=AX.X))
            W("vector", "cV", v_)
            v_ = OP("vector", "cV", lambda eng: eng.reciprocal(out=red[:, 1:2], in_=red[:, 1:2]))
            W("vector", "cV", v_)
            v_ = OP("vector", "cV", lambda eng: eng.memset(zpb[:, :], 0.0))
            W("vector", "cV", v_)
            zpv = OP("vector", "cV", lambda eng: eng.tensor_scalar(
                out=zpb[:, 128:128 + S], in0=z_sb[:, :],
                scalar1=red[:, 1:2], scalar2=None, op0=ALU.mult))

        with contextlib.ExitStack() as pb_stack:
            Eb = pb_stack.enter_context
            v_sb = Eb(nc.sbuf_tensor("v_sb", [128, NT * 1024], BF16))
            Ts = [Eb(nc.sbuf_tensor(f"T{i}", [128, S], BF16)) for i in range(2)]
            conv_q = Eb(nc.sbuf_tensor("conv_q", [128, NCH * 512], BF16))
            pp = [Eb(nc.psum_tensor(f"pp{i}", [128, 512], F32)) for i in range(4)]

            W("sync", "cV", zpv)
            vld = 0
            for J in range(NT):
                vld = DMA("sync", v_sb[:, J * 1024:(J + 1) * 1024],
                          vst.ap()[J * 128:(J + 1) * 128, :], "dS")

            def build_T(h, par):
                src = ap_nd(zpb[h:h + 1, 0:1], 128, [(S + 256, 1), (-1, 128), (1, S)])
                return DMA("gpsimd", Ts[par][:, :], src, "dG")

            W("gpsimd", "cV", zpv)
            tb = {0: build_T(0, 0), 1: build_T(1, 1)}
            W("tensor", "dS", vld)
            W("tensor", "cA", a_end["cA"])
            W("tensor", "cV", a_end["cV"])
            evac_mark = {}
            stash_mark = {}
            for h in range(H):
                par = h % 2
                W("tensor", "dG", tb[h])
                for c in range(NCH):
                    ps = pp[c % 4]
                    prevkey = (h, c - 4) if c >= 4 else (h - 1, c + 4 * ((NCH - 1 - c) // 4))
                    if prevkey in evac_mark:
                        sn, vv = evac_mark[prevkey]
                        W("tensor", sn, vv)
                    mm = 0
                    for d in range(0, 8 * c + 8):
                        if d <= 8 * c:
                            nblk, j0, col0 = 8, 8 * c - d, 0
                        else:
                            nblk, j0, col0 = 8 - (d - 8 * c), 0, (d - 8 * c) * 64
                        rhs = ap_nd(v_sb[0:128, 0:1], j0 * 1024 + h * 64,
                                    [(NT * 1024, 128), (1024, nblk), (1, 64)])
                        mk = (lambda eng, par=par, d=d, ps=ps, rhs=rhs, col0=col0, nblk=nblk, c=c: eng.matmul(
                            ps[:, col0:col0 + nblk * 64],
                            Ts[par][:, d * 128:(d + 1) * 128], rhs,
                            start=(d == 0), stop=(d == 8 * c + 7)))
                        if d == 8 * c + 7:
                            mm = OP("tensor", "cP", mk)
                        else:
                            OPn("tensor", mk)
                    engn = "vector" if c % 2 == 0 else "scalar"
                    sn = "cV" if engn == "vector" else "cA"
                    W(engn, "cP", mm)
                    if (h - 1, c) in stash_mark:
                        W(engn, "dG", stash_mark[(h - 1, c)])
                    if engn == "vector":
                        OP(engn, sn, lambda eng, c=c, ps=ps: eng.tensor_copy(
                            out=conv_q[:, c * 512:(c + 1) * 512], in_=ps[:, :]))
                    else:
                        OP(engn, sn, lambda eng, c=c, ps=ps: eng.activation(
                            out=conv_q[:, c * 512:(c + 1) * 512], in_=ps[:, :], func=AF.Copy))
                    evac_mark[(h, c)] = (sn, cnt[sn])
                    W("gpsimd", sn, cnt[sn])
                    row_g = c * 8 * 128
                    half = row_g // (S // 2)
                    rowin = row_g % (S // 2)
                    dst = cst.ap().copy()
                    dst = ap_nd(dst, half * (S // 2) * D + rowin * D + h * 64,
                                [(1024, 128), (128 * 1024, 8), (1, 64)])
                    csrc = ap_nd(conv_q[0:128, 0:1], c * 512, [(NCH * 512, 128), (64, 8), (1, 64)])
                    stash_mark[(h, c)] = DMA("gpsimd", dst, csrc, "dG")
                if h + 2 < H:
                    pe_done = cnt["cP"]
                    W("gpsimd", "cP", pe_done)
                    tb[h + 2] = build_T(h + 2, par)
        zpb_stack.close()
        b_end = {k: cnt.get(k, 0) for k in ["cP", "cA", "cV", "cG", "dS", "dG"]}

        with contextlib.ExitStack() as pc_stack:
            Ec = pc_stack.enter_context
            cxc = make_ffn_ctx(Ec, "c")
            cfm = Ec(nc.sbuf_tensor("cfm", [128, 8192], BF16))
            ctm_all = Ec(nc.sbuf_tensor("ctm_all", [128, 8192], BF16))
            asb = Ec(nc.sbuf_tensor("asb", [128, 512], F32))
            wo_sb = Ec(nc.sbuf_tensor("wo_sb", [128, KT * 1024], BF16))
            wvt2 = Ec(nc.sbuf_tensor("wvtc", [128, 1024], F32))
            bo_sb = Ec(nc.sbuf_tensor("bo_sb", [128, KT], F32))

            for snm in ["cP", "cA", "cV", "cG", "dG"]:
                W("sync", snm, b_end[snm])
            prev_ca = 0
            for k in range(KT):
                W("sync", "cA", prev_ca)
                ld = DMA("sync", wvt2[:, :], wo[k * 128:(k + 1) * 128, :], "dS")
                W("scalar", "dS", ld)
                prev_ca = OP("scalar", "cA", lambda eng, k=k: eng.activation(
                    out=wo_sb[:, k * 1024:(k + 1) * 1024], in_=wvt2[:, :], func=AF.Copy))
            wo_ready = prev_ca
            bold = DMA("sync", bo_sb[:, :], bo[:, :], "dS")

            def sync_reg_init(eng):
                pid = eng.partition_id()
                eng.reg_mov(reg_sync, pid)
                eng.reg_mod(reg_sync, reg_sync, 2)
            emit("sync", sync_reg_init)

            xf_free = (0, 0, 0, 0)
            cfm_free_pe = 0
            lnb_free_pe = 0
            out_dma = 0
            for m in range((S // 2) // 1024):
                W("sync", "cV", xf_free[0])
                W("sync", "cG", xf_free[1])
                W("sync", "dG", xf_free[2])
                W("sync", "cA", xf_free[3])
                dst_x1 = ap_nd(cxc["xf"][0:128, 0:1], 0,
                               [(8192, 128), (1024, KT), (1, 1024)])

                def ldx1(eng, m=m, dst_ap=dst_x1):
                    idx = eng.snap(reg_sync, min_val=0, max_val=1)
                    src = x1st[idx, 0:128, m * 1024:(m + 1) * 1024]
                    src.ap = bass_rust.VecI64Pair(
                        [(S // 2, 128), (128 * (S // 2), KT), (1, 1024)])
                    return eng.dma_start(out=dst_ap, in_=src)
                cnt["dS"] = cnt.get("dS", 0) + 16
                emit("sync", lambda eng, f=ldx1, s=sems["dS"]: f(eng).then_inc(s, 16))
                x1ld = cnt["dS"]
                emit("sync", lambda eng, s=sems["dS"], v=x1ld: eng.wait_ge(s, v))
                for e in ["vector", "scalar", "tensor", "gpsimd"]:
                    W(e, "dS", x1ld)
                W("vector", "dS", bold)
                bo_cv = 0
                for dm in range(KT):
                    bo_cv = OP("vector", "cV", lambda eng, dm=dm: eng.tensor_scalar(
                        out=cxc["xf"][:, dm * 1024:(dm + 1) * 1024],
                        in0=cxc["xf"][:, dm * 1024:(dm + 1) * 1024],
                        scalar1=bo_sb[:, dm:dm + 1], scalar2=None, op0=ALU.add))
                W("sync", "cP", cxc.get("ctm_free", 0))
                ctm_dst = ap_nd(ctm_all[0:128, 0:1], 0,
                                [(8192, 128), (1024, 8), (1, 1024)])

                def ldc(eng, m=m, ctm_dst=ctm_dst):
                    idx = eng.snap(reg_sync, min_val=0, max_val=1)
                    src = cst[idx, m * 1024:m * 1024 + 128, :]
                    src.ap = bass_rust.VecI64Pair(
                        [(1024, 128), (128 * 1024, 8), (1, 1024)])
                    return eng.dma_start(out=ctm_dst, in_=src)
                cnt["dS"] = cnt.get("dS", 0) + 16
                emit("sync", lambda eng, f=ldc, s=sems["dS"]: f(eng).then_inc(s, 16))
                cld = cnt["dS"]
                emit("sync", lambda eng, s=sems["dS"], v=cld: eng.wait_ge(s, v))
                cfm_cv = 0
                for s8 in range(8):
                    par = s8 % 2
                    pT = cxc["pT"][par]
                    W("tensor", "dS", cld)
                    t1 = t2 = 0
                    cp1 = 0
                    for g in range(KT):
                        mk = (lambda eng, g=g, s8=s8, pT=pT: eng.transpose(
                            pT[:, (g % 4) * 128:(g % 4) * 128 + 128],
                            ctm_all[:, s8 * 1024 + g * 128:s8 * 1024 + g * 128 + 128],
                            ident_sb[:, :]))
                        if g == 3:
                            t1 = OP("tensor", "cP", mk)
                            W("vector", "cP", t1)
                            if cfm_free_pe:
                                W("vector", "cP", cfm_free_pe)
                                cfm_free_pe = 0
                            dst1 = ap_nd(cfm[0:128, 0:1], 0 * 1024 + s8 * 128,
                                         [(8192, 128), (1024, 4), (1, 128)])
                            cp1 = OP("vector", "cV", lambda eng, d=dst1, pT=pT: eng.tensor_copy(
                                out=d, in_=pT[:, :]))
                        elif g == KT - 1:
                            t2 = OP("tensor", "cP", mk)
                        else:
                            if g == 4:
                                W("tensor", "cV", cp1)
                            OPn("tensor", mk)
                    cxc["ctm_free"] = cnt["cP"]
                    W("vector", "cP", t2)
                    dst2 = ap_nd(cfm[0:128, 0:1], 4 * 1024 + s8 * 128,
                                 [(8192, 128), (1024, 4), (1, 128)])
                    cfm_cv = OP("vector", "cV", lambda eng, d=dst2, pT=pT: eng.tensor_copy(out=d, in_=pT[:, :]))
                    W("tensor", "cV", cfm_cv)
                W("tensor", "cV", cfm_cv)
                W("tensor", "cA", wo_ready)
                W("scalar", "dS", bold)
                for dm in range(KT):
                    for half in range(2):
                        pa_ = cxc["pA"] if half == 0 else cxc["pC"]
                        t0 = half * 512
                        amm = 0
                        for k in range(KT):
                            mk = (lambda eng, k=k, dm=dm, pa_=pa_, t0=t0: eng.matmul(
                                pa_[:, :], wo_sb[:, k * 1024 + dm * 128:k * 1024 + dm * 128 + 128],
                                cfm[:, k * 1024 + t0:k * 1024 + t0 + 512],
                                start=(k == 0), stop=(k == KT - 1)))
                            if k == KT - 1:
                                amm = OP("tensor", "cP", mk)
                            else:
                                OPn("tensor", mk)
                        W("scalar", "cP", amm)
                        aev = OP("scalar", "cA", lambda eng, dm=dm, pa_=pa_: eng.activation(
                            out=asb[:, :], in_=pa_[:, :], func=AF.Copy))
                        W("vector", "cA", aev)
                        W("vector", "cV", bo_cv)
                        x2v = OP("vector", "cV", lambda eng, dm=dm, t0=t0: eng.tensor_tensor(
                            out=cxc["xf"][:, dm * 1024 + t0:dm * 1024 + t0 + 512],
                            in0=cxc["xf"][:, dm * 1024 + t0:dm * 1024 + t0 + 512],
                            in1=asb[:, :], op=ALU.add))
                        W("tensor", "cV", x2v)
                        W("scalar", "cV", x2v)
                wo_pe = cnt["cP"]
                cfm_free_pe = wo_pe
                c0 = ln_pass(cxc, 0, pe_clear_v=lnb_free_pe, pe_clear_g=lnb_free_pe,
                             src_cv=cnt["cV"])
                c1 = ln_pass(cxc, 1)
                W("tensor", "cV", max(c0[0], c1[0]))
                W("tensor", "cG", max(c0[1], c1[1]))
                lastcv, _, down_pe = ffn_block(cxc, None, 0.5, stash_x1=None,
                                               xf_read=(max(c0[0], c1[0]), max(c0[1], c1[1])))
                lnb_free_pe = down_pe
                for dm in range(KT):
                    W("gpsimd", "cV", cnt["cV"])
                    out_dma = DMA("gpsimd", out[dm * 128:(dm + 1) * 128, m * 1024:(m + 1) * 1024],
                                  cxc["xf"][:, dm * 1024:(dm + 1) * 1024], "dG")
                xf_free = (cnt["cV"], cnt["cG"], out_dma, cnt["cA"])

            W("gpsimd", "dG", cnt["dG"])

        with nc.Block() as block:
            @block.sync
            def _(eng):
                for fn in sched["sync"]:
                    fn(eng)

            @block.gpsimd
            def _(eng):
                for fn in sched["gpsimd"]:
                    fn(eng)

            @block.scalar
            def _(eng):
                for fn in sched["scalar"]:
                    fn(eng)

            @block.vector
            def _(eng):
                for fn in sched["vector"]:
                    fn(eng)

            @block.tensor
            def _(eng):
                for fn in sched["tensor"]:
                    fn(eng)

    return nc


_NC_CACHE = {}


def kernel(L0_attn_x, L0_attn_W_A, L0_attn_W_V, L0_attn_W_O, L0_attn_b_O,
           L0_ffn_w_gate, L0_ffn_w_up, L0_ffn_w_down):
    x = np.ascontiguousarray(np.asarray(L0_attn_x, dtype=np.float32))
    W_A = np.asarray(L0_attn_W_A, dtype=np.float32)
    W_V = np.asarray(L0_attn_W_V, dtype=np.float32)
    W_O = np.asarray(L0_attn_W_O, dtype=np.float32)
    b_O = np.asarray(L0_attn_b_O, dtype=np.float32)
    wgi = np.asarray(L0_ffn_w_gate, dtype=np.int32)
    wui = np.asarray(L0_ffn_w_up, dtype=np.int32)
    wdi = np.asarray(L0_ffn_w_down, dtype=np.int32)

    def tile_fd(w, FT_, KT_):
        return np.ascontiguousarray(
            w.reshape(FT_, 128, KT_, 128).transpose(0, 3, 2, 1).reshape(FT_ * 128, KT_ * 128))

    wg_s = tile_fd(wgi, FT, KT)
    wu_s = tile_fd(wui, FT, KT)
    wd_s = tile_fd(wdi, KT, FT)
    wv_s = tile_fd(W_V, KT, KT).astype(np.float32)
    wo_s = tile_fd(W_O, KT, KT).astype(np.float32)
    wa_s = np.ascontiguousarray(W_A.T.astype(np.float32))
    bo_s = np.ascontiguousarray(b_O.reshape(KT, 128).T)
    ones_s = np.ones((128, 1), dtype=ml_dtypes.bfloat16)
    ident_s = np.eye(128, dtype=ml_dtypes.bfloat16)

    if "nc" not in _NC_CACHE:
        _NC_CACHE["nc"] = build()
    nc = _NC_CACHE["nc"]

    in_maps = []
    for c in range(8):
        b = c // 2
        in_maps.append(dict(
            xt=np.ascontiguousarray(x[b].T),
            wg=wg_s, wu=wu_s, wd=wd_s, wv=wv_s, wo=wo_s, wa=wa_s, bo=bo_s,
            ones=ones_s, ident=ident_s,
            cstin=np.tile(np.array([[EPS, 0.125]], np.float32), (128, 1)),
        ))
    res = run_bass_kernel_spmd(nc, in_maps, core_ids=list(range(8)))
    outp = np.empty((B, S, D), dtype=np.float32)
    for c in range(8):
        b, r = c // 2, c % 2
        outp[b, r * (S // 2):(r + 1) * (S // 2), :] = np.asarray(res.results[c]["out"]).T
    return outp



# version 04
# speedup vs baseline: 1.1315x; 1.1315x over previous
import contextlib
import numpy as np
import ml_dtypes
import bass_rust
import concourse.bass as bass
import concourse.mybir as mybir
from concourse.bass_utils import run_bass_kernel_spmd

F32 = mybir.dt.float32
BF16 = mybir.dt.bfloat16
I32 = mybir.dt.int32
AF = mybir.ActivationFunctionType
ALU = mybir.AluOpType
AX = mybir.AxisListType

B, S, D = 4, 8192, 1024
H = 16
HID = 3072
KT = D // 128
FT = HID // 128
NT = S // 128
SCALE_W = 1.0 / 127.0
EPS = 1e-5


def ap_nd(base_ap, extra_offset, dims):
    ap = base_ap
    ap.ap = bass_rust.VecI64Pair(dims)
    ap.offset = ap.offset + extra_offset
    return ap


def build():
    nc = bass.Bass("TRN2", target_bir_lowering=False)

    xt = nc.dram_tensor("xt", [D, S], F32, kind="ExternalInput").ap()
    wg = nc.dram_tensor("wg", [HID, D], I32, kind="ExternalInput").ap()
    wu = nc.dram_tensor("wu", [HID, D], I32, kind="ExternalInput").ap()
    wd = nc.dram_tensor("wd", [D, HID], I32, kind="ExternalInput").ap()
    wv = nc.dram_tensor("wv", [D, D], F32, kind="ExternalInput").ap()
    wo = nc.dram_tensor("wo", [D, D], F32, kind="ExternalInput").ap()
    wa = nc.dram_tensor("wa", [D, H], F32, kind="ExternalInput").ap()
    bo = nc.dram_tensor("bo", [128, KT], F32, kind="ExternalInput").ap()
    ones = nc.dram_tensor("ones", [128, 1], BF16, kind="ExternalInput").ap()
    ident = nc.dram_tensor("ident", [128, 128], BF16, kind="ExternalInput").ap()
    cstin = nc.dram_tensor("cstin", [128, 2], F32, kind="ExternalInput").ap()
    out = nc.dram_tensor("out", [D, S // 2], F32, kind="ExternalOutput").ap()

    wsg = nc.dram_tensor("wsg", [HID, D], BF16)
    wsu = nc.dram_tensor("wsu", [HID, D], BF16)
    wsd = nc.dram_tensor("wsd", [D, HID], BF16)
    vst = nc.dram_tensor("vst", [S, D], BF16)
    x1st = nc.dram_tensor("x1st", [2, D, S // 2], F32)
    cst = nc.dram_tensor("cst", [2, S // 2, D], BF16)
    zst = nc.dram_tensor("zst", [H, S], F32)

    sched = {"sync": [], "gpsimd": [], "scalar": [], "vector": [], "tensor": []}
    cnt = {}

    def emit(engname, fn):
        sched[engname].append(fn)

    with contextlib.ExitStack() as top:
        E = top.enter_context
        sems = {s: E(nc.semaphore(s)) for s in ["dS", "dG", "cP", "cA", "cV", "cG"]}

        def W(engname, sem_name, val):
            if val <= 0:
                return
            s = sems[sem_name]
            emit(engname, lambda eng, s=s, v=val: eng.wait_ge(s, v))

        def DMA(engname, dst, src, sem_name):
            cnt[sem_name] = cnt.get(sem_name, 0) + 16
            s = sems[sem_name]
            emit(engname, lambda eng, d=dst, so=src, s=s: eng.dma_start(out=d, in_=so).then_inc(s, 16))
            v = cnt[sem_name]
            emit(engname, lambda eng, s=s, v=v: eng.wait_ge(s, v))
            return cnt[sem_name]

        def DMAn(engname, dst, src, sem_name):
            cnt[sem_name] = cnt.get(sem_name, 0) + 16
            s = sems[sem_name]
            emit(engname, lambda eng, d=dst, so=src, s=s: eng.dma_start(out=d, in_=so).then_inc(s, 16))
            return cnt[sem_name]

        def Wself(engname, sem_name):
            v = cnt.get(sem_name, 0)
            s = sems[sem_name]
            emit(engname, lambda eng, s=s, v=v: eng.wait_ge(s, v))
            return v

        def OP(engname, sem_name, fn):
            cnt[sem_name] = cnt.get(sem_name, 0) + 1
            s = sems[sem_name]
            emit(engname, lambda eng, f=fn, s=s: f(eng).then_inc(s, 1))
            return cnt[sem_name]

        def OPn(engname, fn):
            emit(engname, lambda eng, f=fn: f(eng))

        ident_sb = E(nc.sbuf_tensor("ident_sb", [128, 128], BF16))
        ones_sb = E(nc.sbuf_tensor("ones_sb", [128, 1], BF16))
        csts = E(nc.sbuf_tensor("csts", [128, 2], F32))
        reg_sync = E(nc.sync.register("ridx"))
        reg_holder = {}

        DMA("sync", ident_sb[:, :], ident[:, :], "dS")
        DMA("sync", ones_sb[:, :], ones[:, :], "dS")
        DMA("sync", csts[:, :], cstin[:, :], "dS")

        with contextlib.ExitStack() as p0:
            wis = [p0.enter_context(nc.sbuf_tensor(f"wi{i}", [128, 3072], I32)) for i in range(2)]
            wbs = [p0.enter_context(nc.sbuf_tensor(f"wb{i}", [128, 3072], BF16)) for i in range(2)]
            chunks = []
            for f in range(FT):
                chunks.append((wg, wsg, f * 128, 1024))
                chunks.append((wu, wsu, f * 128, 1024))
            for k in range(KT):
                chunks.append((wd, wsd, k * 128, 3072))
            last_st = {0: 0, 1: 0}
            last_cv = {0: 0, 1: 0}
            for i, (srcd, dstd, r0, ncol) in enumerate(chunks):
                par = i % 2
                W("sync", "cV", last_cv[par])
                ld = DMA("sync", wis[par][:, 0:ncol], srcd[r0:r0 + 128, 0:ncol], "dS")
                W("vector", "dS", ld)
                W("vector", "dG", last_st[par])
                last_cv[par] = OP("vector", "cV", lambda eng, p=par, n=ncol: eng.tensor_scalar(
                    out=wbs[p][:, 0:n], in0=wis[p][:, 0:n],
                    scalar1=SCALE_W, scalar2=None, op0=ALU.mult))
                W("gpsimd", "cV", last_cv[par])
                last_st[par] = DMA("gpsimd", dstd.ap()[r0:r0 + 128, 0:ncol], wbs[par][:, 0:ncol], "dG")
            w_stash_done = cnt["dG"]

        def make_ffn_ctx(Ec, tag):
            ctx = {}
            ctx["xf"] = Ec(nc.sbuf_tensor("xf" + tag, [128, 8192], F32))
            ctx["lnb"] = Ec(nc.sbuf_tensor("lnb" + tag, [128, 8192], BF16))
            ctx["h_sb"] = Ec(nc.sbuf_tensor("h" + tag, [128, FT * 1024], BF16))
            ctx["sg"] = Ec(nc.sbuf_tensor("sg" + tag, [128, 512], BF16))
            ctx["xb"] = Ec(nc.sbuf_tensor("xb" + tag, [128, 1024], BF16))
            ctx["sqb"] = Ec(nc.sbuf_tensor("sqb" + tag, [128, 1024], BF16))
            ctx["tmpn"] = Ec(nc.sbuf_tensor("tmpn" + tag, [128, 512], F32))
            ctx["tmpg"] = Ec(nc.sbuf_tensor("tmpg" + tag, [128, 512], F32))
            ctx["st"] = Ec(nc.sbuf_tensor("st" + tag, [1, 5120], F32))
            ctx["bcb"] = Ec(nc.sbuf_tensor("bcb" + tag, [128, 2048], F32))
            ctx["wgus"] = [Ec(nc.sbuf_tensor(f"wgu{i}" + tag, [128, 2048], BF16)) for i in range(2)]
            ctx["wdbs"] = [Ec(nc.sbuf_tensor(f"wdb{i}" + tag, [128, 3072], BF16)) for i in range(2)]
            ctx["pA"] = Ec(nc.psum_tensor("pA" + tag, [128, 512], F32))
            ctx["pB"] = Ec(nc.psum_tensor("pB" + tag, [128, 512], F32))
            ctx["pC"] = Ec(nc.psum_tensor("pC" + tag, [128, 512], F32))
            ctx["pD"] = Ec(nc.psum_tensor("pD" + tag, [128, 512], F32))
            ctx["pT"] = [Ec(nc.psum_tensor(f"pT{i}" + tag, [128, 512], BF16)) for i in range(2)]
            return ctx

        def ln_pass(cx, half, pe_clear_v=0, pe_clear_g=0, src_cv=0):
            xf, lnb = cx["xf"], cx["lnb"]
            so = half * 512
            sto = half * 2560
            bo_ = half * 1024
            xb = cx["xb"][:, so:so + 512]
            sqb = cx["sqb"][:, so:so + 512]
            tmpn = cx["tmpn"][:, :]
            tmpg = cx["tmpg"][:, :]
            st_ = cx["st"]
            bcb = cx["bcb"][:, bo_:bo_ + 1024]
            class _S:
                def __getitem__(self, idx):
                    a, b = idx[1].start, idx[1].stop
                    return st_[0:1, sto + a:sto + b]
            st = _S()
            pA = cx["pA"] if half == 0 else cx["pC"]
            pB = cx["pB"] if half == 0 else cx["pD"]
            t0 = half * 512
            W("tensor", "cA", cnt.get("cA", 0))
            if src_cv:
                W("vector", "cV", src_cv)
                W("scalar", "cV", src_cv)
                W("gpsimd", "cV", src_cv)
            for k in range(KT):
                sl = slice(k * 1024 + t0, k * 1024 + t0 + 512)
                cv = OP("vector", "cV", lambda eng, sl=sl: eng.tensor_copy(
                    out=xb[:, :], in_=xf[:, sl]))
                ca = OP("scalar", "cA", lambda eng, sl=sl: eng.activation(
                    out=sqb[:, :], in_=xf[:, sl], func=AF.Square))
                W("tensor", "cV", cv)
                W("tensor", "cA", ca)
                v1 = OP("tensor", "cP", lambda eng, k=k: eng.matmul(
                    pA[0:1, :], ones_sb[:, :], xb[:, :], start=(k == 0), stop=(k == KT - 1)))
                v2 = OP("tensor", "cP", lambda eng, k=k: eng.matmul(
                    pB[0:1, :], ones_sb[:, :], sqb[:, :], start=(k == 0), stop=(k == KT - 1)))
                if k < KT - 1:
                    W("vector", "cP", v1)
                    W("scalar", "cP", v2)
            W("vector", "cP", cnt["cP"])
            W("vector", "cV", cx.get(("prev_mr_cv", half), 0))
            v_ = OP("vector", "cV", lambda eng: eng.tensor_scalar(
                out=st[0:1, 0:512], in0=pA[0:1, :], scalar1=1.0 / D, scalar2=None, op0=ALU.mult))
            pa_free = OP("vector", "cV", lambda eng: eng.tensor_scalar(
                out=st[0:1, 1536:2048], in0=pB[0:1, :], scalar1=1.0 / D, scalar2=None, op0=ALU.mult))
            W("vector", "cV", v_)
            v_ = OP("vector", "cV", lambda eng: eng.tensor_tensor(
                out=st[0:1, 2048:2560], in0=st[0:1, 0:512], in1=st[0:1, 0:512], op=ALU.mult))
            W("vector", "cV", v_)
            cv = OP("vector", "cV", lambda eng: eng.tensor_tensor(
                out=st[0:1, 1536:2048], in0=st[0:1, 1536:2048], in1=st[0:1, 2048:2560], op=ALU.subtract))
            W("scalar", "cV", cv)
            ca = OP("scalar", "cA", lambda eng: eng.activation(
                out=st[0:1, 2048:2560], in_=st[0:1, 1536:2048], func=AF.Sqrt, bias=csts[0:1, 0:1]))
            W("vector", "cA", ca)
            v_ = OP("vector", "cV", lambda eng: eng.reciprocal(
                out=st[0:1, 512:1024], in_=st[0:1, 2048:2560]))
            W("vector", "cV", v_)
            cv = OP("vector", "cV", lambda eng: eng.tensor_tensor(
                out=st[0:1, 1024:1536], in0=st[0:1, 0:512], in1=st[0:1, 512:1024], op=ALU.mult))
            cx[("prev_mr_cv", half)] = cv
            W("gpsimd", "cV", cv)
            W("gpsimd", "cV", cx.get(("prev_norm_cv", half), 0))
            W("gpsimd", "cG", cx.get(("prev_norm_cg", half), 0))
            bsrc = ap_nd(st[0:1, 0:1], 512, [(5120, 1), (0, 128), (1, 1024)])
            bc = DMA("gpsimd", bcb[:, :], bsrc, "dG")
            lastcv = cx.get("tmp_cv", 0)
            lastcg = cx.get("tmp_cg", 0)
            W("vector", "dG", bc)
            W("gpsimd", "dG", bc)
            if pe_clear_v:
                W("vector", "cP", pe_clear_v)
            if pe_clear_g:
                W("gpsimd", "cP", pe_clear_g)
            for k in range(KT):
                sl = slice(k * 1024 + t0, k * 1024 + t0 + 512)
                if k % 2 == 0:
                    if lastcv:
                        W("vector", "cV", lastcv)
                    v_ = OP("vector", "cV", lambda eng, sl=sl: eng.tensor_tensor(
                        out=tmpn[:, :], in0=xf[:, sl], in1=bcb[:, 0:512], op=ALU.mult))
                    W("vector", "cV", v_)
                    lastcv = OP("vector", "cV", lambda eng, sl=sl: eng.tensor_tensor(
                        out=lnb[:, sl], in0=tmpn[:, :], in1=bcb[:, 512:1024], op=ALU.subtract))
                else:
                    if lastcg:
                        W("gpsimd", "cG", lastcg)
                    v_ = OP("gpsimd", "cG", lambda eng, sl=sl: eng.tensor_tensor(
                        out=tmpg[:, :], in0=xf[:, sl], in1=bcb[:, 0:512], op=ALU.mult))
                    W("gpsimd", "cG", v_)
                    lastcg = OP("gpsimd", "cG", lambda eng, sl=sl: eng.tensor_tensor(
                        out=lnb[:, sl], in0=tmpg[:, :], in1=bcb[:, 512:1024], op=ALU.subtract))
            cx[("prev_norm_cv", half)] = lastcv
            cx[("prev_norm_cg", half)] = lastcg
            cx["tmp_cv"] = lastcv
            cx["tmp_cg"] = lastcg
            return lastcv, lastcg, pa_free

        def ffn_block(cx, wload_gate, half_scale, stash_x1=None, xf_read=(0, 0)):
            lnb, h_sb, sg, xf = cx["lnb"], cx["h_sb"], cx["sg"], cx["xf"]
            wgus, wdbs = cx["wgus"], cx["wdbs"]
            pA, pB, pC, pD = cx["pA"], cx["pB"], cx["pC"], cx["pD"]
            prev_sgv = {}
            prev_hd = {}
            for f in range(FT):
                par = f % 2
                W("sync", "cP", cx.get("wgu_free", {}).get(par, 0))
                DMAn("sync", wgus[par][:, 0:1024], wsg.ap()[f * 128:(f + 1) * 128, :], "dS")
                l2 = DMAn("sync", wgus[par][:, 1024:2048], wsu.ap()[f * 128:(f + 1) * 128, :], "dS")
                Wself("sync", "dS")
                W("tensor", "dS", l2)
                for half in range(2):
                    pg = pA if half == 0 else pC
                    pu = pB if half == 0 else pD
                    t0 = half * 512
                    if (half,) in prev_sgv:
                        W("tensor", "cA", prev_sgv[(half,)])
                        W("tensor", "cV", prev_hd[(half,)])
                    gate_done = up_done = 0
                    for k in range(KT):
                        mk = (lambda eng, k=k, par=par, pg=pg, t0=t0: eng.matmul(
                            pg[:, :], wgus[par][:, k * 128:(k + 1) * 128],
                            lnb[:, k * 1024 + t0:k * 1024 + t0 + 512],
                            start=(k == 0), stop=(k == KT - 1)))
                        if k == KT - 1:
                            gate_done = OP("tensor", "cP", mk)
                        else:
                            OPn("tensor", mk)
                    for k in range(KT):
                        mk = (lambda eng, k=k, par=par, pu=pu, t0=t0: eng.matmul(
                            pu[:, :], wgus[par][:, 1024 + k * 128:1024 + (k + 1) * 128],
                            lnb[:, k * 1024 + t0:k * 1024 + t0 + 512],
                            start=(k == 0), stop=(k == KT - 1)))
                        if k == KT - 1:
                            up_done = OP("tensor", "cP", mk)
                        else:
                            OPn("tensor", mk)
                    W("scalar", "cP", gate_done)
                    W("scalar", "cV", cx.get("prev_hdone", 0))
                    sgv = OP("scalar", "cA", lambda eng, pg=pg: eng.activation(
                        out=sg[:, :], in_=pg[:, :], func=AF.Silu))
                    W("vector", "cP", up_done)
                    W("vector", "cA", sgv)
                    hdone = OP("vector", "cV", lambda eng, f=f, half=half, pu=pu: eng.tensor_tensor(
                        out=h_sb[:, f * 1024 + half * 512:f * 1024 + half * 512 + 512],
                        in0=sg[:, :], in1=pu[:, :], op=ALU.mult))
                    cx["prev_hdone"] = hdone
                    prev_sgv[(half,)] = sgv
                    prev_hd[(half,)] = hdone
                cx.setdefault("wgu_free", {})[par] = cnt["cP"]
            h_all = cnt["cV"]
            last_dma = 0
            for dm in range(KT):
                par = dm % 2
                W("sync", "cP", cx.get("wdb_free", {}).get(par, 0))
                l = DMA("sync", wdbs[par][:, :], wsd.ap()[dm * 128:(dm + 1) * 128, :], "dS")
                W("tensor", "dS", l)
                W("tensor", "cV", h_all)
                for half in range(2):
                    pd = pA if half == 0 else pC
                    t0 = half * 512
                    dn = 0
                    if dm == 0 and half == 0:
                        W("vector", "cV", xf_read[0])
                        W("vector", "cG", xf_read[1])
                    if dm == 0:
                        W("tensor", "cA", prev_sgv[(half,)])
                    else:
                        W("tensor", "cV", cx.get("prev_x1v", {}).get(half, 0))
                    for f in range(FT):
                        mk = (lambda eng, f=f, par=par, pd=pd, half=half: eng.matmul(
                            pd[:, :], wdbs[par][:, f * 128:(f + 1) * 128],
                            h_sb[:, f * 1024 + half * 512:f * 1024 + half * 512 + 512],
                            start=(f == 0), stop=(f == FT - 1)))
                        if f == FT - 1:
                            dn = OP("tensor", "cP", mk)
                        else:
                            OPn("tensor", mk)
                    W("vector", "cP", dn)
                    x1v = OP("vector", "cV", lambda eng, dm=dm, t0=t0, pd=pd, hs=half_scale: eng.scalar_tensor_tensor(
                        out=xf[:, dm * 1024 + t0:dm * 1024 + t0 + 512],
                        in0=pd[:, :], scalar=hs,
                        in1=xf[:, dm * 1024 + t0:dm * 1024 + t0 + 512],
                        op0=ALU.mult, op1=ALU.add))
                    cx.setdefault("prev_x1v", {})[half] = x1v
                cx.setdefault("wdb_free", {})[par] = cnt["cP"]
                if stash_x1 is not None:
                    W("gpsimd", "cV", cnt["cV"])
                    last_dma = DMAn("gpsimd", stash_x1(dm), xf[:, dm * 1024:(dm + 1) * 1024], "dG")
            if stash_x1 is not None:
                last_dma = Wself("gpsimd", "dG")
            return cnt["cV"], last_dma, cnt["cP"]

        NMACRO = S // 1024
        HFM = max(NMACRO // 2, 1)
        NCH = NT // 8
        with contextlib.ExitStack() as pa_stack:
            Ea = pa_stack.enter_context
            cx = make_ffn_ctx(Ea, "a")
            vf = Ea(nc.sbuf_tensor("vf", [128, 4096], BF16))
            vts = [Ea(nc.sbuf_tensor(f"vt{i}", [128, 1024], BF16)) for i in range(2)]
            wv_sb = Ea(nc.sbuf_tensor("wv_sb", [128, KT * 1024], BF16))
            zstg = Ea(nc.sbuf_tensor("zstg", [H, 512], F32))
            wa_sb = Ea(nc.sbuf_tensor("wa_sb", [128, KT * H], BF16))
            wvt = Ea(nc.sbuf_tensor("wvt", [128, 1024], F32))

            W("sync", "cV", cnt["cV"])
            W("sync", "dG", w_stash_done)
            prev_ca = 0
            for k in range(KT):
                W("sync", "cA", prev_ca)
                ld = DMA("sync", wvt[:, :], wv[k * 128:(k + 1) * 128, :], "dS")
                W("scalar", "dS", ld)
                prev_ca = OP("scalar", "cA", lambda eng, k=k: eng.activation(
                    out=wv_sb[:, k * 1024:(k + 1) * 1024], in_=wvt[:, :], func=AF.Copy))
            wa_src = wa.copy()
            wa_src.ap = bass_rust.VecI64Pair([(H, 128), (128 * H, KT), (1, H)])
            wa_src.offset = 0
            W("sync", "cA", prev_ca)
            wadst = ap_nd(wvt[0:128, 0:1], 0, [(1024, 128), (H, KT), (1, H)])
            ld = DMA("sync", wadst, wa_src, "dS")
            W("scalar", "dS", ld)
            wa_ready = OP("scalar", "cA", lambda eng: eng.activation(
                out=wa_sb[:, :], in_=wvt[:, 0:KT * H], func=AF.Copy))

            vz_pe_done = 0
            lnb_free_pe = 0
            xf_free = (0, 0, 0, 0)
            vt_dma = {0: 0, 1: 0}
            for m in range(NMACRO):
                W("sync", "cV", xf_free[0])
                W("sync", "cG", xf_free[1])
                W("sync", "dG", xf_free[2])
                W("sync", "cA", xf_free[3])
                if m == 0:
                    W("sync", "dG", w_stash_done)
                xld = 0
                for k in range(KT):
                    xld = DMAn("sync", cx["xf"][:, k * 1024:(k + 1) * 1024],
                               xt[k * 128:(k + 1) * 128, m * 1024:(m + 1) * 1024], "dS")
                Wself("sync", "dS")
                for e in ["vector", "scalar", "tensor", "gpsimd"]:
                    W(e, "dS", xld)
                c0 = ln_pass(cx, 0, pe_clear_v=lnb_free_pe, pe_clear_g=lnb_free_pe)
                c1 = ln_pass(cx, 1)
                W("tensor", "cV", max(c0[0], c1[0]))
                W("tensor", "cG", max(c0[1], c1[1]))
                hf = m // HFM

                def stash(dm, m=m, hf=hf):
                    return x1st.ap()[hf, dm * 128:(dm + 1) * 128,
                                     (m % HFM) * 1024:(m % HFM) * 1024 + 1024]
                lastcv, x1dma, down_pe = ffn_block(cx, None, 0.5, stash_x1=stash,
                                                   xf_read=(max(c0[0], c1[0]), max(c0[1], c1[1])))
                c0 = ln_pass(cx, 0, pe_clear_v=down_pe, pe_clear_g=down_pe, src_cv=lastcv)
                c1 = ln_pass(cx, 1)
                W("tensor", "cV", max(c0[0], c1[0]))
                W("tensor", "cG", max(c0[1], c1[1]))
                W("tensor", "cA", wa_ready)
                lnbA = cx["lnb"]
                pDA = cx["pD"]
                for half in range(2):
                    t0 = half * 512
                    vev_slot = cx.setdefault("vev_slot", {})
                    for fv in range(KT):
                        pv = [cx["pA"], cx["pC"], cx["pB"], cx["pD"]][fv % 4]
                        if (fv % 4) in vev_slot:
                            W("tensor", "cA", vev_slot[fv % 4])
                        elif fv == 0:
                            W("tensor", "cV", cx.get("prev_x1v", {}).get(1, 0))
                        vmm = 0
                        for k in range(KT):
                            mk = (lambda eng, fv=fv, k=k, pv=pv, t0=t0: eng.matmul(
                                pv[:, :], wv_sb[:, k * 1024 + fv * 128:k * 1024 + fv * 128 + 128],
                                lnbA[:, k * 1024 + t0:k * 1024 + t0 + 512],
                                start=(k == 0), stop=(k == KT - 1)))
                            if k == KT - 1:
                                vmm = OP("tensor", "cP", mk)
                            else:
                                OPn("tensor", mk)
                        W("scalar", "cP", vmm)
                        vev = OP("scalar", "cA", lambda eng, fv=fv, pv=pv: eng.activation(
                            out=vf[:, fv * 512:(fv + 1) * 512], in_=pv[:, :], func=AF.Copy))
                        vev_slot[fv % 4] = vev
                    W("tensor", "cA", cx["vev_slot"][3])
                    zmm = 0
                    for k in range(KT):
                        mk = (lambda eng, k=k, t0=t0: eng.matmul(
                            pDA[0:16, :], wa_sb[:, k * H:(k + 1) * H],
                            lnbA[:, k * 1024 + t0:k * 1024 + t0 + 512],
                            start=(k == 0), stop=(k == KT - 1)))
                        if k == KT - 1:
                            zmm = OP("tensor", "cP", mk)
                        else:
                            OPn("tensor", mk)
                    W("scalar", "cP", zmm)
                    W("scalar", "dG", cx.get("zdma", 0))
                    zev = OP("scalar", "cA", lambda eng: eng.activation(
                        out=zstg[:, :], in_=pDA[0:16, :], func=AF.Copy,
                        scale=csts[0:16, 1:2]))
                    W("tensor", "cA", zev)
                    W("gpsimd", "cA", zev)
                    cx["zdma"] = DMA("gpsimd", zst.ap()[0:H, m * 1024 + t0:m * 1024 + t0 + 512],
                                     zstg[:, :], "dG")
                    for s4 in range(4):
                        par = s4 % 2
                        vt = vts[par]
                        pT = cx["pT"][par]
                        t1 = t2 = 0
                        cp1 = 0
                        for fv in range(KT):
                            mk = (lambda eng, fv=fv, s4=s4, pT=pT: eng.transpose(
                                pT[:, (fv % 4) * 128:(fv % 4) * 128 + 128],
                                vf[:, fv * 512 + s4 * 128:fv * 512 + s4 * 128 + 128],
                                ident_sb[:, :]))
                            if fv == 3:
                                t1 = OP("tensor", "cP", mk)
                                W("vector", "cP", t1)
                                W("vector", "dG", vt_dma[par])
                                cp1 = OP("vector", "cV", lambda eng, vt=vt, pT=pT: eng.tensor_copy(
                                    out=vt[:, 0:512], in_=pT[:, :]))
                            elif fv == KT - 1:
                                t2 = OP("tensor", "cP", mk)
                            else:
                                if fv == 4:
                                    W("tensor", "cV", cp1)
                                OPn("tensor", mk)
                        W("vector", "cP", t2)
                        cvv = OP("vector", "cV", lambda eng, vt=vt, pT=pT: eng.tensor_copy(
                            out=vt[:, 512:1024], in_=pT[:, :]))
                        W("gpsimd", "cV", cvv)
                        row0 = (m * 8 + half * 4 + s4) * 128
                        vt_dma[par] = DMA("gpsimd", vst.ap()[row0:row0 + 128, :], vt[:, :], "dG")
                        W("tensor", "cV", cvv)
                vz_pe_done = cnt["cP"]
                lnb_free_pe = vz_pe_done
                xf_free = (cnt["cV"], cnt["cG"], x1dma, cnt["cA"])
            a_end = {k: cnt.get(k, 0) for k in ["cP", "cA", "cV", "cG", "dS", "dG"]}

        zpb_stack = contextlib.ExitStack()
        zpb = zpb_stack.enter_context(nc.sbuf_tensor("zpb", [H, S + 256], BF16))
        with contextlib.ExitStack() as pbs:
            red = pbs.enter_context(nc.sbuf_tensor("red", [H, 2], F32))
            z_sb = pbs.enter_context(nc.sbuf_tensor("z_sb", [H, S], F32))
            W("sync", "dG", a_end["dG"])
            W("sync", "cP", a_end["cP"])
            W("sync", "cA", a_end["cA"])
            W("sync", "cV", a_end["cV"])
            W("sync", "cG", a_end["cG"])
            zld = DMA("sync", z_sb[:, :], zst.ap()[:, :], "dS")
            W("vector", "dS", zld)
            W("vector", "cA", a_end["cA"])
            rv = OP("vector", "cV", lambda eng: eng.reduce_max(
                out=red[:, 0:1], in_=z_sb[:, :], axis=AX.X, negate=True))
            W("scalar", "cV", rv)
            ev = OP("scalar", "cA", lambda eng: eng.activation(
                out=z_sb[:, :], in_=z_sb[:, :], func=AF.Exp, bias=red[:, 0:1]))
            W("vector", "cA", ev)
            v_ = OP("vector", "cV", lambda eng: eng.reduce_sum(
                out=red[:, 1:2], in_=z_sb[:, :], axis=AX.X))
            W("vector", "cV", v_)
            v_ = OP("vector", "cV", lambda eng: eng.reciprocal(out=red[:, 1:2], in_=red[:, 1:2]))
            W("vector", "cV", v_)
            v_ = OP("vector", "cV", lambda eng: eng.memset(zpb[:, :], 0.0))
            W("vector", "cV", v_)
            zpv = OP("vector", "cV", lambda eng: eng.tensor_scalar(
                out=zpb[:, 128:128 + S], in0=z_sb[:, :],
                scalar1=red[:, 1:2], scalar2=None, op0=ALU.mult))

        with contextlib.ExitStack() as pb_stack:
            Eb = pb_stack.enter_context
            v_sb = Eb(nc.sbuf_tensor("v_sb", [128, NT * 1024], BF16))
            Ts = [Eb(nc.sbuf_tensor(f"T{i}", [128, S], BF16)) for i in range(2)]
            conv_q = Eb(nc.sbuf_tensor("conv_q", [128, NCH * 512], BF16))
            pp = [Eb(nc.psum_tensor(f"pp{i}", [128, 512], F32)) for i in range(4)]

            W("sync", "cV", zpv)
            vld = 0
            for J in range(NT):
                vld = DMAn("sync", v_sb[:, J * 1024:(J + 1) * 1024],
                           vst.ap()[J * 128:(J + 1) * 128, :], "dS")
                if J % 8 == 7:
                    Wself("sync", "dS")

            def build_T(h, par):
                src = ap_nd(zpb[h:h + 1, 0:1], 128, [(S + 256, 1), (-1, 128), (1, S)])
                return DMA("gpsimd", Ts[par][:, :], src, "dG")

            W("gpsimd", "cV", zpv)
            tb = {0: build_T(0, 0), 1: build_T(1, 1)}
            W("tensor", "dS", vld)
            W("tensor", "cA", a_end["cA"])
            W("tensor", "cV", a_end["cV"])
            evac_mark = {}
            stash_mark = {}
            for h in range(H):
                par = h % 2
                W("tensor", "dG", tb[h])
                for c in range(NCH):
                    ps = pp[c % 4]
                    prevkey = (h, c - 4) if c >= 4 else (h - 1, c + 4 * ((NCH - 1 - c) // 4))
                    if prevkey in evac_mark:
                        sn, vv = evac_mark[prevkey]
                        W("tensor", sn, vv)
                    mm = 0
                    for d in range(0, 8 * c + 8):
                        if d <= 8 * c:
                            nblk, j0, col0 = 8, 8 * c - d, 0
                        else:
                            nblk, j0, col0 = 8 - (d - 8 * c), 0, (d - 8 * c) * 64
                        rhs = ap_nd(v_sb[0:128, 0:1], j0 * 1024 + h * 64,
                                    [(NT * 1024, 128), (1024, nblk), (1, 64)])
                        mk = (lambda eng, par=par, d=d, ps=ps, rhs=rhs, col0=col0, nblk=nblk, c=c: eng.matmul(
                            ps[:, col0:col0 + nblk * 64],
                            Ts[par][:, d * 128:(d + 1) * 128], rhs,
                            start=(d == 0), stop=(d == 8 * c + 7)))
                        if d == 8 * c + 7:
                            mm = OP("tensor", "cP", mk)
                        else:
                            OPn("tensor", mk)
                    engn = "vector" if c % 2 == 0 else "scalar"
                    sn = "cV" if engn == "vector" else "cA"
                    W(engn, "cP", mm)
                    if (h - 1, c) in stash_mark:
                        W(engn, "dG", stash_mark[(h - 1, c)])
                    if engn == "vector":
                        OP(engn, sn, lambda eng, c=c, ps=ps: eng.tensor_copy(
                            out=conv_q[:, c * 512:(c + 1) * 512], in_=ps[:, :]))
                    else:
                        OP(engn, sn, lambda eng, c=c, ps=ps: eng.activation(
                            out=conv_q[:, c * 512:(c + 1) * 512], in_=ps[:, :], func=AF.Copy))
                    evac_mark[(h, c)] = (sn, cnt[sn])
                    W("gpsimd", sn, cnt[sn])
                    row_g = c * 8 * 128
                    half = row_g // (S // 2)
                    rowin = row_g % (S // 2)
                    dst = cst.ap().copy()
                    dst = ap_nd(dst, half * (S // 2) * D + rowin * D + h * 64,
                                [(1024, 128), (128 * 1024, 8), (1, 64)])
                    csrc = ap_nd(conv_q[0:128, 0:1], c * 512, [(NCH * 512, 128), (64, 8), (1, 64)])
                    DMAn("gpsimd", dst, csrc, "dG")
                    if c == NCH - 1:
                        htot = Wself("gpsimd", "dG")
                        for cc in range(NCH):
                            stash_mark[(h, cc)] = htot
                if h + 2 < H:
                    pe_done = cnt["cP"]
                    W("gpsimd", "cP", pe_done)
                    tb[h + 2] = build_T(h + 2, par)
        zpb_stack.close()
        b_end = {k: cnt.get(k, 0) for k in ["cP", "cA", "cV", "cG", "dS", "dG"]}

        with contextlib.ExitStack() as pc_stack:
            Ec = pc_stack.enter_context
            cxc = make_ffn_ctx(Ec, "c")
            cfm = Ec(nc.sbuf_tensor("cfm", [128, 8192], BF16))
            ctm_all = Ec(nc.sbuf_tensor("ctm_all", [128, 8192], BF16))
            asb = Ec(nc.sbuf_tensor("asb", [128, 512], F32))
            wo_sb = Ec(nc.sbuf_tensor("wo_sb", [128, KT * 1024], BF16))
            bo_sb = Ec(nc.sbuf_tensor("bo_sb", [128, KT], F32))

            for snm in ["cP", "cA", "cV", "cG", "dG"]:
                W("sync", snm, b_end[snm])
            with nc.sbuf_tensor("wvtc", [128, 1024], F32) as wvt2:
                prev_ca = 0
                for k in range(KT):
                    W("sync", "cA", prev_ca)
                    ld = DMA("sync", wvt2[:, :], wo[k * 128:(k + 1) * 128, :], "dS")
                    W("scalar", "dS", ld)
                    prev_ca = OP("scalar", "cA", lambda eng, k=k: eng.activation(
                        out=wo_sb[:, k * 1024:(k + 1) * 1024], in_=wvt2[:, :], func=AF.Copy))
            wo_ready = prev_ca
            bold = DMA("sync", bo_sb[:, :], bo[:, :], "dS")

            def sync_reg_init(eng):
                pid = eng.partition_id()
                eng.reg_mov(reg_sync, pid)
                eng.reg_mod(reg_sync, reg_sync, 2)
            emit("sync", sync_reg_init)

            xf_free = (0, 0, 0, 0)
            cfm_free_pe = 0
            lnb_free_pe = 0
            out_dma = 0
            for m in range((S // 2) // 1024):
                W("sync", "cV", xf_free[0])
                W("sync", "cG", xf_free[1])
                W("sync", "dG", xf_free[2])
                W("sync", "cA", xf_free[3])
                dst_x1 = ap_nd(cxc["xf"][0:128, 0:1], 0,
                               [(8192, 128), (1024, KT), (1, 1024)])

                def ldx1(eng, m=m, dst_ap=dst_x1):
                    idx = eng.snap(reg_sync, min_val=0, max_val=1)
                    src = x1st[idx, 0:128, m * 1024:(m + 1) * 1024]
                    src.ap = bass_rust.VecI64Pair(
                        [(S // 2, 128), (128 * (S // 2), KT), (1, 1024)])
                    return eng.dma_start(out=dst_ap, in_=src)
                cnt["dS"] = cnt.get("dS", 0) + 16
                emit("sync", lambda eng, f=ldx1, s=sems["dS"]: f(eng).then_inc(s, 16))
                x1ld = cnt["dS"]
                emit("sync", lambda eng, s=sems["dS"], v=x1ld: eng.wait_ge(s, v))
                for e in ["vector", "scalar", "tensor", "gpsimd"]:
                    W(e, "dS", x1ld)
                W("vector", "dS", bold)
                bo_cv = 0
                for dm in range(KT):
                    bo_cv = OP("vector", "cV", lambda eng, dm=dm: eng.tensor_scalar(
                        out=cxc["xf"][:, dm * 1024:(dm + 1) * 1024],
                        in0=cxc["xf"][:, dm * 1024:(dm + 1) * 1024],
                        scalar1=bo_sb[:, dm:dm + 1], scalar2=None, op0=ALU.add))
                W("sync", "cP", cxc.get("ctm_free", 0))
                ctm_dst = ap_nd(ctm_all[0:128, 0:1], 0,
                                [(8192, 128), (1024, 8), (1, 1024)])

                def ldc(eng, m=m, ctm_dst=ctm_dst):
                    idx = eng.snap(reg_sync, min_val=0, max_val=1)
                    src = cst[idx, m * 1024:m * 1024 + 128, :]
                    src.ap = bass_rust.VecI64Pair(
                        [(1024, 128), (128 * 1024, 8), (1, 1024)])
                    return eng.dma_start(out=ctm_dst, in_=src)
                cnt["dS"] = cnt.get("dS", 0) + 16
                emit("sync", lambda eng, f=ldc, s=sems["dS"]: f(eng).then_inc(s, 16))
                cld = cnt["dS"]
                emit("sync", lambda eng, s=sems["dS"], v=cld: eng.wait_ge(s, v))
                cfm_cv = 0
                for s8 in range(8):
                    par = s8 % 2
                    pT = cxc["pT"][par]
                    W("tensor", "dS", cld)
                    t1 = t2 = 0
                    cp1 = 0
                    for g in range(KT):
                        mk = (lambda eng, g=g, s8=s8, pT=pT: eng.transpose(
                            pT[:, (g % 4) * 128:(g % 4) * 128 + 128],
                            ctm_all[:, s8 * 1024 + g * 128:s8 * 1024 + g * 128 + 128],
                            ident_sb[:, :]))
                        if g == 3:
                            t1 = OP("tensor", "cP", mk)
                            W("vector", "cP", t1)
                            if cfm_free_pe:
                                W("vector", "cP", cfm_free_pe)
                                cfm_free_pe = 0
                            dst1 = ap_nd(cfm[0:128, 0:1], 0 * 1024 + s8 * 128,
                                         [(8192, 128), (1024, 4), (1, 128)])
                            cp1 = OP("vector", "cV", lambda eng, d=dst1, pT=pT: eng.tensor_copy(
                                out=d, in_=pT[:, :]))
                        elif g == KT - 1:
                            t2 = OP("tensor", "cP", mk)
                        else:
                            if g == 4:
                                W("tensor", "cV", cp1)
                            OPn("tensor", mk)
                    cxc["ctm_free"] = cnt["cP"]
                    W("vector", "cP", t2)
                    dst2 = ap_nd(cfm[0:128, 0:1], 4 * 1024 + s8 * 128,
                                 [(8192, 128), (1024, 4), (1, 128)])
                    cfm_cv = OP("vector", "cV", lambda eng, d=dst2, pT=pT: eng.tensor_copy(out=d, in_=pT[:, :]))
                    W("tensor", "cV", cfm_cv)
                W("tensor", "cV", cfm_cv)
                W("tensor", "cA", wo_ready)
                W("scalar", "dS", bold)
                for dm in range(KT):
                    for half in range(2):
                        pa_ = cxc["pA"] if half == 0 else cxc["pC"]
                        t0 = half * 512
                        amm = 0
                        for k in range(KT):
                            mk = (lambda eng, k=k, dm=dm, pa_=pa_, t0=t0: eng.matmul(
                                pa_[:, :], wo_sb[:, k * 1024 + dm * 128:k * 1024 + dm * 128 + 128],
                                cfm[:, k * 1024 + t0:k * 1024 + t0 + 512],
                                start=(k == 0), stop=(k == KT - 1)))
                            if k == KT - 1:
                                amm = OP("tensor", "cP", mk)
                            else:
                                OPn("tensor", mk)
                        W("scalar", "cP", amm)
                        aev = OP("scalar", "cA", lambda eng, dm=dm, pa_=pa_: eng.activation(
                            out=asb[:, :], in_=pa_[:, :], func=AF.Copy))
                        W("vector", "cA", aev)
                        W("vector", "cV", bo_cv)
                        x2v = OP("vector", "cV", lambda eng, dm=dm, t0=t0: eng.tensor_tensor(
                            out=cxc["xf"][:, dm * 1024 + t0:dm * 1024 + t0 + 512],
                            in0=cxc["xf"][:, dm * 1024 + t0:dm * 1024 + t0 + 512],
                            in1=asb[:, :], op=ALU.add))
                        W("tensor", "cV", x2v)
                        W("scalar", "cV", x2v)
                wo_pe = cnt["cP"]
                cfm_free_pe = wo_pe
                c0 = ln_pass(cxc, 0, pe_clear_v=lnb_free_pe, pe_clear_g=lnb_free_pe,
                             src_cv=cnt["cV"])
                c1 = ln_pass(cxc, 1)
                W("tensor", "cV", max(c0[0], c1[0]))
                W("tensor", "cG", max(c0[1], c1[1]))
                lastcv, _, down_pe = ffn_block(cxc, None, 0.5, stash_x1=None,
                                               xf_read=(max(c0[0], c1[0]), max(c0[1], c1[1])))
                lnb_free_pe = down_pe
                for dm in range(KT):
                    W("gpsimd", "cV", cnt["cV"])
                    out_dma = DMA("gpsimd", out[dm * 128:(dm + 1) * 128, m * 1024:(m + 1) * 1024],
                                  cxc["xf"][:, dm * 1024:(dm + 1) * 1024], "dG")
                xf_free = (cnt["cV"], cnt["cG"], out_dma, cnt["cA"])

            W("gpsimd", "dG", cnt["dG"])

        with nc.Block() as block:
            @block.sync
            def _(eng):
                for fn in sched["sync"]:
                    fn(eng)

            @block.gpsimd
            def _(eng):
                for fn in sched["gpsimd"]:
                    fn(eng)

            @block.scalar
            def _(eng):
                for fn in sched["scalar"]:
                    fn(eng)

            @block.vector
            def _(eng):
                for fn in sched["vector"]:
                    fn(eng)

            @block.tensor
            def _(eng):
                for fn in sched["tensor"]:
                    fn(eng)

    return nc


_NC_CACHE = {}


def kernel(L0_attn_x, L0_attn_W_A, L0_attn_W_V, L0_attn_W_O, L0_attn_b_O,
           L0_ffn_w_gate, L0_ffn_w_up, L0_ffn_w_down):
    x = np.ascontiguousarray(np.asarray(L0_attn_x, dtype=np.float32))
    W_A = np.asarray(L0_attn_W_A, dtype=np.float32)
    W_V = np.asarray(L0_attn_W_V, dtype=np.float32)
    W_O = np.asarray(L0_attn_W_O, dtype=np.float32)
    b_O = np.asarray(L0_attn_b_O, dtype=np.float32)
    wgi = np.asarray(L0_ffn_w_gate, dtype=np.int32)
    wui = np.asarray(L0_ffn_w_up, dtype=np.int32)
    wdi = np.asarray(L0_ffn_w_down, dtype=np.int32)

    def tile_fd(w, FT_, KT_):
        return np.ascontiguousarray(
            w.reshape(FT_, 128, KT_, 128).transpose(0, 3, 2, 1).reshape(FT_ * 128, KT_ * 128))

    wg_s = tile_fd(wgi, FT, KT)
    wu_s = tile_fd(wui, FT, KT)
    wd_s = tile_fd(wdi, KT, FT)
    wv_s = tile_fd(W_V, KT, KT).astype(np.float32)
    wo_s = tile_fd(W_O, KT, KT).astype(np.float32)
    wa_s = np.ascontiguousarray(W_A.T.astype(np.float32))
    bo_s = np.ascontiguousarray(b_O.reshape(KT, 128).T)
    ones_s = np.ones((128, 1), dtype=ml_dtypes.bfloat16)
    ident_s = np.eye(128, dtype=ml_dtypes.bfloat16)

    if "nc" not in _NC_CACHE:
        _NC_CACHE["nc"] = build()
    nc = _NC_CACHE["nc"]

    in_maps = []
    for c in range(8):
        b = c // 2
        in_maps.append(dict(
            xt=np.ascontiguousarray(x[b].T),
            wg=wg_s, wu=wu_s, wd=wd_s, wv=wv_s, wo=wo_s, wa=wa_s, bo=bo_s,
            ones=ones_s, ident=ident_s,
            cstin=np.tile(np.array([[EPS, 0.125]], np.float32), (128, 1)),
        ))
    res = run_bass_kernel_spmd(nc, in_maps, core_ids=list(range(8)))
    outp = np.empty((B, S, D), dtype=np.float32)
    for c in range(8):
        b, r = c // 2, c % 2
        outp[b, r * (S // 2):(r + 1) * (S // 2), :] = np.asarray(res.results[c]["out"]).T
    return outp



# version 06 (final)
# speedup vs baseline: 1.1530x; 1.0189x over previous
import contextlib
import numpy as np
import ml_dtypes
import bass_rust
import concourse.bass as bass
import concourse.mybir as mybir
from concourse.bass_utils import run_bass_kernel_spmd

F32 = mybir.dt.float32
BF16 = mybir.dt.bfloat16
I32 = mybir.dt.int32
AF = mybir.ActivationFunctionType
ALU = mybir.AluOpType
AX = mybir.AxisListType

B, S, D = 4, 8192, 1024
H = 16
HID = 3072
KT = D // 128
FT = HID // 128
NT = S // 128
SCALE_W = 1.0 / 127.0
EPS = 1e-5


def ap_nd(base_ap, extra_offset, dims):
    ap = base_ap
    ap.ap = bass_rust.VecI64Pair(dims)
    ap.offset = ap.offset + extra_offset
    return ap


def build():
    nc = bass.Bass("TRN2", target_bir_lowering=False)

    xt = nc.dram_tensor("xt", [D, S], F32, kind="ExternalInput").ap()
    wg = nc.dram_tensor("wg", [HID, D], I32, kind="ExternalInput").ap()
    wu = nc.dram_tensor("wu", [HID, D], I32, kind="ExternalInput").ap()
    wd = nc.dram_tensor("wd", [D, HID], I32, kind="ExternalInput").ap()
    wv = nc.dram_tensor("wv", [D, D], F32, kind="ExternalInput").ap()
    wo = nc.dram_tensor("wo", [D, D], F32, kind="ExternalInput").ap()
    wa = nc.dram_tensor("wa", [D, H], F32, kind="ExternalInput").ap()
    bo = nc.dram_tensor("bo", [128, KT], F32, kind="ExternalInput").ap()
    ones = nc.dram_tensor("ones", [128, 1], BF16, kind="ExternalInput").ap()
    ident = nc.dram_tensor("ident", [128, 128], BF16, kind="ExternalInput").ap()
    cstin = nc.dram_tensor("cstin", [128, 2], F32, kind="ExternalInput").ap()
    out = nc.dram_tensor("out", [D, S // 2], F32, kind="ExternalOutput").ap()

    wsg = nc.dram_tensor("wsg", [HID, D], BF16)
    wsu = nc.dram_tensor("wsu", [HID, D], BF16)
    wsd = nc.dram_tensor("wsd", [D, HID], BF16)
    vst = nc.dram_tensor("vst", [S, D], BF16)
    x1st = nc.dram_tensor("x1st", [2, D, S // 2], F32)
    cst = nc.dram_tensor("cst", [2, S // 2, D], BF16)
    zst = nc.dram_tensor("zst", [H, S], F32)

    sched = {"sync": [], "gpsimd": [], "scalar": [], "vector": [], "tensor": []}
    cnt = {}

    def emit(engname, fn):
        sched[engname].append(fn)

    with contextlib.ExitStack() as top:
        E = top.enter_context
        sems = {s: E(nc.semaphore(s)) for s in ["dS", "dG", "cP", "cA", "cV", "cG"]}

        def W(engname, sem_name, val):
            if val <= 0:
                return
            s = sems[sem_name]
            emit(engname, lambda eng, s=s, v=val: eng.wait_ge(s, v))

        def DMA(engname, dst, src, sem_name):
            cnt[sem_name] = cnt.get(sem_name, 0) + 16
            s = sems[sem_name]
            emit(engname, lambda eng, d=dst, so=src, s=s: eng.dma_start(out=d, in_=so).then_inc(s, 16))
            v = cnt[sem_name]
            emit(engname, lambda eng, s=s, v=v: eng.wait_ge(s, v))
            return cnt[sem_name]

        def DMAn(engname, dst, src, sem_name):
            cnt[sem_name] = cnt.get(sem_name, 0) + 16
            s = sems[sem_name]
            emit(engname, lambda eng, d=dst, so=src, s=s: eng.dma_start(out=d, in_=so).then_inc(s, 16))
            return cnt[sem_name]

        def Wself(engname, sem_name):
            v = cnt.get(sem_name, 0)
            s = sems[sem_name]
            emit(engname, lambda eng, s=s, v=v: eng.wait_ge(s, v))
            return v

        def OP(engname, sem_name, fn):
            cnt[sem_name] = cnt.get(sem_name, 0) + 1
            s = sems[sem_name]
            emit(engname, lambda eng, f=fn, s=s: f(eng).then_inc(s, 1))
            return cnt[sem_name]

        def OPn(engname, fn):
            emit(engname, lambda eng, f=fn: f(eng))

        ident_sb = E(nc.sbuf_tensor("ident_sb", [128, 128], BF16))
        ones_sb = E(nc.sbuf_tensor("ones_sb", [128, 1], BF16))
        csts = E(nc.sbuf_tensor("csts", [128, 2], F32))
        reg_sync = E(nc.sync.register("ridx"))
        reg_holder = {}

        DMA("sync", ident_sb[:, :], ident[:, :], "dS")
        DMA("sync", ones_sb[:, :], ones[:, :], "dS")
        DMA("sync", csts[:, :], cstin[:, :], "dS")

        with contextlib.ExitStack() as p0:
            wis = [p0.enter_context(nc.sbuf_tensor(f"wi{i}", [128, 3072], I32)) for i in range(2)]
            wbs = [p0.enter_context(nc.sbuf_tensor(f"wb{i}", [128, 3072], BF16)) for i in range(2)]
            chunks = []
            for f in range(FT):
                chunks.append((wg, wsg, f * 128, 1024))
                chunks.append((wu, wsu, f * 128, 1024))
            for k in range(KT):
                chunks.append((wd, wsd, k * 128, 3072))
            last_st = {0: 0, 1: 0}
            last_cv = {0: 0, 1: 0}
            for i, (srcd, dstd, r0, ncol) in enumerate(chunks):
                par = i % 2
                W("sync", "cV", last_cv[par])
                ld = DMA("sync", wis[par][:, 0:ncol], srcd[r0:r0 + 128, 0:ncol], "dS")
                W("vector", "dS", ld)
                W("vector", "dG", last_st[par])
                last_cv[par] = OP("vector", "cV", lambda eng, p=par, n=ncol: eng.tensor_scalar(
                    out=wbs[p][:, 0:n], in0=wis[p][:, 0:n],
                    scalar1=SCALE_W, scalar2=None, op0=ALU.mult))
                W("gpsimd", "cV", last_cv[par])
                last_st[par] = DMA("gpsimd", dstd.ap()[r0:r0 + 128, 0:ncol], wbs[par][:, 0:ncol], "dG")
            w_stash_done = cnt["dG"]

        def make_ffn_ctx(Ec, tag):
            ctx = {}
            ctx["xf"] = Ec(nc.sbuf_tensor("xf" + tag, [128, 8192], F32))
            ctx["lnb"] = Ec(nc.sbuf_tensor("lnb" + tag, [128, 8192], BF16))
            ctx["h_sb"] = Ec(nc.sbuf_tensor("h" + tag, [128, FT * 1024], BF16))
            ctx["sg"] = Ec(nc.sbuf_tensor("sg" + tag, [128, 512], BF16))
            ctx["xb"] = Ec(nc.sbuf_tensor("xb" + tag, [128, 1024], BF16))
            ctx["sqb"] = Ec(nc.sbuf_tensor("sqb" + tag, [128, 1024], BF16))
            ctx["tmpn"] = Ec(nc.sbuf_tensor("tmpn" + tag, [128, 512], F32))
            ctx["tmpg"] = Ec(nc.sbuf_tensor("tmpg" + tag, [128, 512], F32))
            ctx["st"] = Ec(nc.sbuf_tensor("st" + tag, [1, 5120], F32))
            ctx["bcb"] = Ec(nc.sbuf_tensor("bcb" + tag, [128, 2048], F32))
            ctx["wgus"] = [Ec(nc.sbuf_tensor(f"wgu{i}" + tag, [128, 2048], BF16)) for i in range(2)]
            ctx["wdbs"] = [Ec(nc.sbuf_tensor(f"wdb{i}" + tag, [128, 3072], BF16)) for i in range(2)]
            ctx["pA"] = Ec(nc.psum_tensor("pA" + tag, [128, 512], F32))
            ctx["pB"] = Ec(nc.psum_tensor("pB" + tag, [128, 512], F32))
            ctx["pC"] = Ec(nc.psum_tensor("pC" + tag, [128, 512], F32))
            ctx["pD"] = Ec(nc.psum_tensor("pD" + tag, [128, 512], F32))
            ctx["pT"] = [Ec(nc.psum_tensor(f"pT{i}" + tag, [128, 512], BF16)) for i in range(2)]
            return ctx

        def ln_pass(cx, half, src_cv=0):
            xf = cx["xf"]
            ev = "vector" if half == 0 else "gpsimd"
            sv = "cV" if half == 0 else "cG"
            so = half * 512
            sto = half * 2560
            xb = cx["xb"][:, so:so + 512]
            sqb = cx["sqb"][:, so:so + 512]
            st_ = cx["st"]
            bcb = cx["bcb"]
            class _S:
                def __getitem__(self, idx):
                    a, b = idx[1].start, idx[1].stop
                    return st_[0:1, sto + a:sto + b]
            st = _S()
            pA = cx["pA"] if half == 0 else cx["pC"]
            pB = cx["pB"] if half == 0 else cx["pD"]
            t0 = half * 512
            if src_cv:
                W(ev, "cV", src_cv)
                W("scalar", "cV", src_cv)
            W("tensor", "cA", cnt.get("cA", 0))
            for k in range(KT):
                sl = slice(k * 1024 + t0, k * 1024 + t0 + 512)
                cv = OP(ev, sv, lambda eng, sl=sl: eng.tensor_copy(
                    out=xb[:, :], in_=xf[:, sl]))
                ca = OP("scalar", "cA", lambda eng, sl=sl: eng.activation(
                    out=sqb[:, :], in_=xf[:, sl], func=AF.Square))
                W("tensor", sv, cv)
                W("tensor", "cA", ca)
                v1 = OP("tensor", "cP", lambda eng, k=k: eng.matmul(
                    pA[0:1, :], ones_sb[:, :], xb[:, :], start=(k == 0), stop=(k == KT - 1)))
                v2 = OP("tensor", "cP", lambda eng, k=k: eng.matmul(
                    pB[0:1, :], ones_sb[:, :], sqb[:, :], start=(k == 0), stop=(k == KT - 1)))
                if k < KT - 1:
                    W(ev, "cP", v1)
                    W("scalar", "cP", v2)
            W("vector", "cP", cnt["cP"])
            W("vector", "cV", cx.get(("prev_mr", half), 0))
            v_ = OP("vector", "cV", lambda eng: eng.tensor_scalar(
                out=st[0:1, 0:512], in0=pA[0:1, :], scalar1=1.0 / D, scalar2=None, op0=ALU.mult))
            pa_free = OP("vector", "cV", lambda eng: eng.tensor_scalar(
                out=st[0:1, 1536:2048], in0=pB[0:1, :], scalar1=1.0 / D, scalar2=None, op0=ALU.mult))
            W(ev, "cV", v_)
            v_ = OP(ev, sv, lambda eng: eng.tensor_tensor(
                out=st[0:1, 2048:2560], in0=st[0:1, 0:512], in1=st[0:1, 0:512], op=ALU.mult))
            W(ev, sv, v_)
            W(ev, "cV", pa_free)
            cv = OP(ev, sv, lambda eng: eng.tensor_tensor(
                out=st[0:1, 1536:2048], in0=st[0:1, 1536:2048], in1=st[0:1, 2048:2560], op=ALU.subtract))
            W("scalar", sv, cv)
            ca = OP("scalar", "cA", lambda eng: eng.activation(
                out=st[0:1, 2048:2560], in_=st[0:1, 1536:2048], func=AF.Sqrt, bias=csts[0:1, 0:1]))
            W("vector", "cA", ca)
            W("vector", "cV", cx.get(("prev_mr", half), 0))
            v_ = OP("vector", "cV", lambda eng: eng.reciprocal(
                out=st[0:1, 512:1024], in_=st[0:1, 2048:2560]))
            W("vector", "cV", v_)
            cv = OP("vector", "cV", lambda eng: eng.tensor_tensor(
                out=st[0:1, 1024:1536], in0=st[0:1, 0:512], in1=st[0:1, 512:1024], op=ALU.mult))
            cx[("prev_mr", half)] = cv
            W("gpsimd", "cV", cv)
            W("gpsimd", sv, cx.get(("bc_free", half), 0))
            bsrc1 = ap_nd(st[0:1, 0:1], 512, [(5120, 1), (0, 128), (1, 512)])
            DMAn("gpsimd", bcb[:, half * 512:half * 512 + 512], bsrc1, "dG")
            bsrc2 = ap_nd(st[0:1, 0:1], 1024, [(5120, 1), (0, 128), (1, 512)])
            bc = DMAn("gpsimd", bcb[:, 1024 + half * 512:1024 + half * 512 + 512], bsrc2, "dG")
            Wself("gpsimd", "dG")
            return bc, pa_free

        def ln_norm(cx, bc0, bc1, pe_clear=0):
            xf, lnb, bcb = cx["xf"], cx["lnb"], cx["bcb"]
            lastcv = cx.get("tmp_cv", 0)
            lastcg = cx.get("tmp_cg", 0)
            W("vector", "dG", max(bc0, bc1))
            W("gpsimd", "dG", max(bc0, bc1))
            if pe_clear:
                W("vector", "cP", pe_clear)
                W("gpsimd", "cP", pe_clear)
            for k in range(KT):
                sl = slice(k * 1024, (k + 1) * 1024)
                ev = "vector" if k % 2 == 0 else "gpsimd"
                sv = "cV" if k % 2 == 0 else "cG"
                v_ = OP(ev, sv, lambda eng, sl=sl: eng.tensor_tensor(
                    out=lnb[:, sl], in0=xf[:, sl], in1=bcb[:, 0:1024], op=ALU.mult))
                W(ev, sv, v_)
                m_ = OP(ev, sv, lambda eng, sl=sl: eng.tensor_tensor(
                    out=lnb[:, sl], in0=lnb[:, sl], in1=bcb[:, 1024:2048], op=ALU.subtract))
                if k % 2 == 0:
                    lastcv = m_
                else:
                    lastcg = m_
            cx["tmp_cv"] = lastcv
            cx["tmp_cg"] = lastcg
            cx[("bc_free", 0)] = lastcv
            cx[("bc_free", 1)] = lastcg
            return lastcv, lastcg

        def ffn_block(cx, wload_gate, half_scale, stash_x1=None, xf_read=(0, 0)):
            lnb, h_sb, sg, xf = cx["lnb"], cx["h_sb"], cx["sg"], cx["xf"]
            wgus, wdbs = cx["wgus"], cx["wdbs"]
            pA, pB, pC, pD = cx["pA"], cx["pB"], cx["pC"], cx["pD"]
            prev_sgv = {}
            prev_hd = {}
            for f in range(FT):
                par = f % 2
                W("sync", "cP", cx.get("wgu_free", {}).get(par, 0))
                DMAn("sync", wgus[par][:, 0:1024], wsg.ap()[f * 128:(f + 1) * 128, :], "dS")
                l2 = DMAn("sync", wgus[par][:, 1024:2048], wsu.ap()[f * 128:(f + 1) * 128, :], "dS")
                Wself("sync", "dS")
                W("tensor", "dS", l2)
                for half in range(2):
                    pg = pA if half == 0 else pC
                    pu = pB if half == 0 else pD
                    t0 = half * 512
                    if (half,) in prev_sgv:
                        W("tensor", "cA", prev_sgv[(half,)])
                        W("tensor", "cV", prev_hd[(half,)])
                    gate_done = up_done = 0
                    for k in range(KT):
                        mk = (lambda eng, k=k, par=par, pg=pg, t0=t0: eng.matmul(
                            pg[:, :], wgus[par][:, k * 128:(k + 1) * 128],
                            lnb[:, k * 1024 + t0:k * 1024 + t0 + 512],
                            start=(k == 0), stop=(k == KT - 1)))
                        if k == KT - 1:
                            gate_done = OP("tensor", "cP", mk)
                        else:
                            OPn("tensor", mk)
                    for k in range(KT):
                        mk = (lambda eng, k=k, par=par, pu=pu, t0=t0: eng.matmul(
                            pu[:, :], wgus[par][:, 1024 + k * 128:1024 + (k + 1) * 128],
                            lnb[:, k * 1024 + t0:k * 1024 + t0 + 512],
                            start=(k == 0), stop=(k == KT - 1)))
                        if k == KT - 1:
                            up_done = OP("tensor", "cP", mk)
                        else:
                            OPn("tensor", mk)
                    W("scalar", "cP", gate_done)
                    W("scalar", "cV", cx.get("prev_hdone", 0))
                    sgv = OP("scalar", "cA", lambda eng, pg=pg: eng.activation(
                        out=sg[:, :], in_=pg[:, :], func=AF.Silu))
                    W("vector", "cP", up_done)
                    W("vector", "cA", sgv)
                    hdone = OP("vector", "cV", lambda eng, f=f, half=half, pu=pu: eng.tensor_tensor(
                        out=h_sb[:, f * 1024 + half * 512:f * 1024 + half * 512 + 512],
                        in0=sg[:, :], in1=pu[:, :], op=ALU.mult))
                    cx["prev_hdone"] = hdone
                    prev_sgv[(half,)] = sgv
                    prev_hd[(half,)] = hdone
                cx.setdefault("wgu_free", {})[par] = cnt["cP"]
            h_all = cnt["cV"]
            last_dma = 0
            for dm in range(KT):
                par = dm % 2
                W("sync", "cP", cx.get("wdb_free", {}).get(par, 0))
                l = DMA("sync", wdbs[par][:, :], wsd.ap()[dm * 128:(dm + 1) * 128, :], "dS")
                W("tensor", "dS", l)
                W("tensor", "cV", h_all)
                for half in range(2):
                    pd = pA if half == 0 else pC
                    t0 = half * 512
                    dn = 0
                    if dm == 0 and half == 0:
                        W("vector", "cV", xf_read[0])
                        W("vector", "cG", xf_read[1])
                    if dm == 0:
                        W("tensor", "cA", prev_sgv[(half,)])
                    else:
                        W("tensor", "cV", cx.get("prev_x1v", {}).get(half, 0))
                    for f in range(FT):
                        mk = (lambda eng, f=f, par=par, pd=pd, half=half: eng.matmul(
                            pd[:, :], wdbs[par][:, f * 128:(f + 1) * 128],
                            h_sb[:, f * 1024 + half * 512:f * 1024 + half * 512 + 512],
                            start=(f == 0), stop=(f == FT - 1)))
                        if f == FT - 1:
                            dn = OP("tensor", "cP", mk)
                        else:
                            OPn("tensor", mk)
                    W("vector", "cP", dn)
                    x1v = OP("vector", "cV", lambda eng, dm=dm, t0=t0, pd=pd, hs=half_scale: eng.scalar_tensor_tensor(
                        out=xf[:, dm * 1024 + t0:dm * 1024 + t0 + 512],
                        in0=pd[:, :], scalar=hs,
                        in1=xf[:, dm * 1024 + t0:dm * 1024 + t0 + 512],
                        op0=ALU.mult, op1=ALU.add))
                    cx.setdefault("prev_x1v", {})[half] = x1v
                cx.setdefault("wdb_free", {})[par] = cnt["cP"]
                if stash_x1 is not None:
                    W("gpsimd", "cV", cnt["cV"])
                    last_dma = DMAn("gpsimd", stash_x1(dm), xf[:, dm * 1024:(dm + 1) * 1024], "dG")
            if stash_x1 is not None:
                last_dma = Wself("gpsimd", "dG")
            return cnt["cV"], last_dma, cnt["cP"]

        NMACRO = S // 1024
        HFM = max(NMACRO // 2, 1)
        NCH = NT // 8
        with contextlib.ExitStack() as pa_stack:
            Ea = pa_stack.enter_context
            cx = make_ffn_ctx(Ea, "a")
            vf = Ea(nc.sbuf_tensor("vf", [128, 4096], BF16))
            vts = [Ea(nc.sbuf_tensor(f"vt{i}", [128, 1024], BF16)) for i in range(2)]
            wv_sb = Ea(nc.sbuf_tensor("wv_sb", [128, KT * 1024], BF16))
            zstg = Ea(nc.sbuf_tensor("zstg", [H, 512], F32))
            wa_sb = Ea(nc.sbuf_tensor("wa_sb", [128, KT * H], BF16))
            wvt = Ea(nc.sbuf_tensor("wvt", [128, 1024], F32))

            W("sync", "cV", cnt["cV"])
            W("sync", "dG", w_stash_done)
            prev_ca = 0
            for k in range(KT):
                W("sync", "cA", prev_ca)
                ld = DMA("sync", wvt[:, :], wv[k * 128:(k + 1) * 128, :], "dS")
                W("scalar", "dS", ld)
                prev_ca = OP("scalar", "cA", lambda eng, k=k: eng.activation(
                    out=wv_sb[:, k * 1024:(k + 1) * 1024], in_=wvt[:, :], func=AF.Copy))
            wa_src = wa.copy()
            wa_src.ap = bass_rust.VecI64Pair([(H, 128), (128 * H, KT), (1, H)])
            wa_src.offset = 0
            W("sync", "cA", prev_ca)
            wadst = ap_nd(wvt[0:128, 0:1], 0, [(1024, 128), (H, KT), (1, H)])
            ld = DMA("sync", wadst, wa_src, "dS")
            W("scalar", "dS", ld)
            wa_ready = OP("scalar", "cA", lambda eng: eng.activation(
                out=wa_sb[:, :], in_=wvt[:, 0:KT * H], func=AF.Copy))

            vz_pe_done = 0
            lnb_free_pe = 0
            xf_free = (0, 0, 0, 0)
            vt_dma = {0: 0, 1: 0}
            for m in range(NMACRO):
                W("sync", "cV", xf_free[0])
                W("sync", "cG", xf_free[1])
                W("sync", "dG", xf_free[2])
                W("sync", "cA", xf_free[3])
                if m == 0:
                    W("sync", "dG", w_stash_done)
                xld = 0
                for k in range(KT):
                    xld = DMAn("sync", cx["xf"][:, k * 1024:(k + 1) * 1024],
                               xt[k * 128:(k + 1) * 128, m * 1024:(m + 1) * 1024], "dS")
                Wself("sync", "dS")
                for e in ["vector", "scalar", "tensor", "gpsimd"]:
                    W(e, "dS", xld)
                b0 = ln_pass(cx, 0)
                b1 = ln_pass(cx, 1)
                c0 = ln_norm(cx, b0[0], b1[0], pe_clear=lnb_free_pe)
                W("tensor", "cV", c0[0])
                W("tensor", "cG", c0[1])
                hf = m // HFM

                def stash(dm, m=m, hf=hf):
                    return x1st.ap()[hf, dm * 128:(dm + 1) * 128,
                                     (m % HFM) * 1024:(m % HFM) * 1024 + 1024]
                lastcv, x1dma, down_pe = ffn_block(cx, None, 0.5, stash_x1=stash,
                                                   xf_read=(c0[0], c0[1]))
                b0 = ln_pass(cx, 0, src_cv=lastcv)
                b1 = ln_pass(cx, 1, src_cv=lastcv)
                c0 = ln_norm(cx, b0[0], b1[0], pe_clear=down_pe)
                W("tensor", "cV", c0[0])
                W("tensor", "cG", c0[1])
                W("tensor", "cA", wa_ready)
                lnbA = cx["lnb"]
                pDA = cx["pD"]
                for half in range(2):
                    t0 = half * 512
                    vev_slot = cx.setdefault("vev_slot", {})
                    for fv in range(KT):
                        pv = [cx["pA"], cx["pC"], cx["pB"], cx["pD"]][fv % 4]
                        if (fv % 4) in vev_slot:
                            W("tensor", "cA", vev_slot[fv % 4])
                        elif fv == 0:
                            W("tensor", "cV", cx.get("prev_x1v", {}).get(1, 0))
                        vmm = 0
                        for k in range(KT):
                            mk = (lambda eng, fv=fv, k=k, pv=pv, t0=t0: eng.matmul(
                                pv[:, :], wv_sb[:, k * 1024 + fv * 128:k * 1024 + fv * 128 + 128],
                                lnbA[:, k * 1024 + t0:k * 1024 + t0 + 512],
                                start=(k == 0), stop=(k == KT - 1)))
                            if k == KT - 1:
                                vmm = OP("tensor", "cP", mk)
                            else:
                                OPn("tensor", mk)
                        W("scalar", "cP", vmm)
                        vev = OP("scalar", "cA", lambda eng, fv=fv, pv=pv: eng.activation(
                            out=vf[:, fv * 512:(fv + 1) * 512], in_=pv[:, :], func=AF.Copy))
                        vev_slot[fv % 4] = vev
                    W("tensor", "cA", cx["vev_slot"][3])
                    zmm = 0
                    for k in range(KT):
                        mk = (lambda eng, k=k, t0=t0: eng.matmul(
                            pDA[0:16, :], wa_sb[:, k * H:(k + 1) * H],
                            lnbA[:, k * 1024 + t0:k * 1024 + t0 + 512],
                            start=(k == 0), stop=(k == KT - 1)))
                        if k == KT - 1:
                            zmm = OP("tensor", "cP", mk)
                        else:
                            OPn("tensor", mk)
                    W("scalar", "cP", zmm)
                    W("scalar", "dG", cx.get("zdma", 0))
                    zev = OP("scalar", "cA", lambda eng: eng.activation(
                        out=zstg[:, :], in_=pDA[0:16, :], func=AF.Copy,
                        scale=csts[0:16, 1:2]))
                    W("tensor", "cA", zev)
                    W("gpsimd", "cA", zev)
                    cx["zdma"] = DMA("gpsimd", zst.ap()[0:H, m * 1024 + t0:m * 1024 + t0 + 512],
                                     zstg[:, :], "dG")
                    for s4 in range(4):
                        par = s4 % 2
                        vt = vts[par]
                        pT = cx["pT"][par]
                        t1 = t2 = 0
                        cp1 = 0
                        for fv in range(KT):
                            mk = (lambda eng, fv=fv, s4=s4, pT=pT: eng.transpose(
                                pT[:, (fv % 4) * 128:(fv % 4) * 128 + 128],
                                vf[:, fv * 512 + s4 * 128:fv * 512 + s4 * 128 + 128],
                                ident_sb[:, :]))
                            if fv == 3:
                                t1 = OP("tensor", "cP", mk)
                                W("vector", "cP", t1)
                                W("vector", "dG", vt_dma[par])
                                cp1 = OP("vector", "cV", lambda eng, vt=vt, pT=pT: eng.tensor_copy(
                                    out=vt[:, 0:512], in_=pT[:, :]))
                            elif fv == KT - 1:
                                t2 = OP("tensor", "cP", mk)
                            else:
                                if fv == 4:
                                    W("tensor", "cV", cp1)
                                OPn("tensor", mk)
                        W("vector", "cP", t2)
                        cvv = OP("vector", "cV", lambda eng, vt=vt, pT=pT: eng.tensor_copy(
                            out=vt[:, 512:1024], in_=pT[:, :]))
                        W("gpsimd", "cV", cvv)
                        row0 = (m * 8 + half * 4 + s4) * 128
                        vt_dma[par] = DMA("gpsimd", vst.ap()[row0:row0 + 128, :], vt[:, :], "dG")
                        W("tensor", "cV", cvv)
                vz_pe_done = cnt["cP"]
                lnb_free_pe = vz_pe_done
                xf_free = (cnt["cV"], cnt["cG"], x1dma, cnt["cA"])
            a_end = {k: cnt.get(k, 0) for k in ["cP", "cA", "cV", "cG", "dS", "dG"]}

        zpb_stack = contextlib.ExitStack()
        zpb = zpb_stack.enter_context(nc.sbuf_tensor("zpb", [H, S + 256], BF16))
        with contextlib.ExitStack() as pbs:
            red = pbs.enter_context(nc.sbuf_tensor("red", [H, 2], F32))
            z_sb = pbs.enter_context(nc.sbuf_tensor("z_sb", [H, S], F32))
            W("sync", "dG", a_end["dG"])
            W("sync", "cP", a_end["cP"])
            W("sync", "cA", a_end["cA"])
            W("sync", "cV", a_end["cV"])
            W("sync", "cG", a_end["cG"])
            zld = DMA("sync", z_sb[:, :], zst.ap()[:, :], "dS")
            W("vector", "dS", zld)
            W("vector", "cA", a_end["cA"])
            rv = OP("vector", "cV", lambda eng: eng.reduce_max(
                out=red[:, 0:1], in_=z_sb[:, :], axis=AX.X, negate=True))
            W("scalar", "cV", rv)
            ev = OP("scalar", "cA", lambda eng: eng.activation(
                out=z_sb[:, :], in_=z_sb[:, :], func=AF.Exp, bias=red[:, 0:1]))
            W("vector", "cA", ev)
            v_ = OP("vector", "cV", lambda eng: eng.reduce_sum(
                out=red[:, 1:2], in_=z_sb[:, :], axis=AX.X))
            W("vector", "cV", v_)
            v_ = OP("vector", "cV", lambda eng: eng.reciprocal(out=red[:, 1:2], in_=red[:, 1:2]))
            W("vector", "cV", v_)
            v_ = OP("vector", "cV", lambda eng: eng.memset(zpb[:, :], 0.0))
            W("vector", "cV", v_)
            zpv = OP("vector", "cV", lambda eng: eng.tensor_scalar(
                out=zpb[:, 128:128 + S], in0=z_sb[:, :],
                scalar1=red[:, 1:2], scalar2=None, op0=ALU.mult))

        with contextlib.ExitStack() as pb_stack:
            Eb = pb_stack.enter_context
            v_sb = Eb(nc.sbuf_tensor("v_sb", [128, NT * 1024], BF16))
            Ts = [Eb(nc.sbuf_tensor(f"T{i}", [128, S], BF16)) for i in range(2)]
            conv_q = Eb(nc.sbuf_tensor("conv_q", [128, NCH * 512], BF16))
            pp = [Eb(nc.psum_tensor(f"pp{i}", [128, 512], F32)) for i in range(4)]

            W("sync", "cV", zpv)
            vld = 0
            for J in range(NT):
                vld = DMAn("sync", v_sb[:, J * 1024:(J + 1) * 1024],
                           vst.ap()[J * 128:(J + 1) * 128, :], "dS")
                if J % 8 == 7:
                    Wself("sync", "dS")

            def build_T(h, par):
                src = ap_nd(zpb[h:h + 1, 0:1], 128, [(S + 256, 1), (-1, 128), (1, S)])
                return DMA("gpsimd", Ts[par][:, :], src, "dG")

            W("gpsimd", "cV", zpv)
            tb = {0: build_T(0, 0), 1: build_T(1, 1)}
            W("tensor", "dS", vld)
            W("tensor", "cA", a_end["cA"])
            W("tensor", "cV", a_end["cV"])
            evac_mark = {}
            stash_mark = {}
            for h in range(H):
                par = h % 2
                W("tensor", "dG", tb[h])
                for c in range(NCH):
                    ps = pp[c % 4]
                    prevkey = (h, c - 4) if c >= 4 else (h - 1, c + 4 * ((NCH - 1 - c) // 4))
                    if prevkey in evac_mark:
                        sn, vv = evac_mark[prevkey]
                        W("tensor", sn, vv)
                    mm = 0
                    for d in range(0, 8 * c + 8):
                        if d <= 8 * c:
                            nblk, j0, col0 = 8, 8 * c - d, 0
                        else:
                            nblk, j0, col0 = 8 - (d - 8 * c), 0, (d - 8 * c) * 64
                        rhs = ap_nd(v_sb[0:128, 0:1], j0 * 1024 + h * 64,
                                    [(NT * 1024, 128), (1024, nblk), (1, 64)])
                        mk = (lambda eng, par=par, d=d, ps=ps, rhs=rhs, col0=col0, nblk=nblk, c=c: eng.matmul(
                            ps[:, col0:col0 + nblk * 64],
                            Ts[par][:, d * 128:(d + 1) * 128], rhs,
                            start=(d == 0), stop=(d == 8 * c + 7)))
                        if d == 8 * c + 7:
                            mm = OP("tensor", "cP", mk)
                        else:
                            OPn("tensor", mk)
                    engn = "vector" if c % 2 == 0 else "scalar"
                    sn = "cV" if engn == "vector" else "cA"
                    W(engn, "cP", mm)
                    if (h - 1, c) in stash_mark:
                        W(engn, "dG", stash_mark[(h - 1, c)])
                    if engn == "vector":
                        OP(engn, sn, lambda eng, c=c, ps=ps: eng.tensor_copy(
                            out=conv_q[:, c * 512:(c + 1) * 512], in_=ps[:, :]))
                    else:
                        OP(engn, sn, lambda eng, c=c, ps=ps: eng.activation(
                            out=conv_q[:, c * 512:(c + 1) * 512], in_=ps[:, :], func=AF.Copy))
                    evac_mark[(h, c)] = (sn, cnt[sn])
                    W("gpsimd", sn, cnt[sn])
                    row_g = c * 8 * 128
                    half = row_g // (S // 2)
                    rowin = row_g % (S // 2)
                    dst = cst.ap().copy()
                    dst = ap_nd(dst, half * (S // 2) * D + rowin * D + h * 64,
                                [(1024, 128), (128 * 1024, 8), (1, 64)])
                    csrc = ap_nd(conv_q[0:128, 0:1], c * 512, [(NCH * 512, 128), (64, 8), (1, 64)])
                    DMAn("gpsimd", dst, csrc, "dG")
                    if c == NCH - 1:
                        htot = Wself("gpsimd", "dG")
                        for cc in range(NCH):
                            stash_mark[(h, cc)] = htot
                if h + 2 < H:
                    pe_done = cnt["cP"]
                    W("gpsimd", "cP", pe_done)
                    tb[h + 2] = build_T(h + 2, par)
        zpb_stack.close()
        b_end = {k: cnt.get(k, 0) for k in ["cP", "cA", "cV", "cG", "dS", "dG"]}

        with contextlib.ExitStack() as pc_stack:
            Ec = pc_stack.enter_context
            cxc = make_ffn_ctx(Ec, "c")
            cfm = Ec(nc.sbuf_tensor("cfm", [128, 8192], BF16))
            ctm_all = Ec(nc.sbuf_tensor("ctm_all", [128, 8192], BF16))
            asb = Ec(nc.sbuf_tensor("asb", [128, 512], F32))
            wo_sb = Ec(nc.sbuf_tensor("wo_sb", [128, KT * 1024], BF16))
            bo_sb = Ec(nc.sbuf_tensor("bo_sb", [128, KT], F32))

            for snm in ["cP", "cA", "cV", "cG", "dG"]:
                W("sync", snm, b_end[snm])
            with nc.sbuf_tensor("wvtc", [128, 1024], F32) as wvt2:
                prev_ca = 0
                for k in range(KT):
                    W("sync", "cA", prev_ca)
                    ld = DMA("sync", wvt2[:, :], wo[k * 128:(k + 1) * 128, :], "dS")
                    W("scalar", "dS", ld)
                    prev_ca = OP("scalar", "cA", lambda eng, k=k: eng.activation(
                        out=wo_sb[:, k * 1024:(k + 1) * 1024], in_=wvt2[:, :], func=AF.Copy))
            wo_ready = prev_ca
            bold = DMA("sync", bo_sb[:, :], bo[:, :], "dS")

            def sync_reg_init(eng):
                pid = eng.partition_id()
                eng.reg_mov(reg_sync, pid)
                eng.reg_mod(reg_sync, reg_sync, 2)
            emit("sync", sync_reg_init)

            xf_free = (0, 0, 0, 0)
            cfm_free_pe = 0
            lnb_free_pe = 0
            out_dma = 0
            for m in range((S // 2) // 1024):
                W("sync", "cV", xf_free[0])
                W("sync", "cG", xf_free[1])
                W("sync", "dG", xf_free[2])
                W("sync", "cA", xf_free[3])
                dst_x1 = ap_nd(cxc["xf"][0:128, 0:1], 0,
                               [(8192, 128), (1024, KT), (1, 1024)])

                def ldx1(eng, m=m, dst_ap=dst_x1):
                    idx = eng.snap(reg_sync, min_val=0, max_val=1)
                    src = x1st[idx, 0:128, m * 1024:(m + 1) * 1024]
                    src.ap = bass_rust.VecI64Pair(
                        [(S // 2, 128), (128 * (S // 2), KT), (1, 1024)])
                    return eng.dma_start(out=dst_ap, in_=src)
                cnt["dS"] = cnt.get("dS", 0) + 16
                emit("sync", lambda eng, f=ldx1, s=sems["dS"]: f(eng).then_inc(s, 16))
                x1ld = cnt["dS"]
                emit("sync", lambda eng, s=sems["dS"], v=x1ld: eng.wait_ge(s, v))
                for e in ["vector", "scalar", "tensor", "gpsimd"]:
                    W(e, "dS", x1ld)
                W("vector", "dS", bold)
                bo_cv = 0
                for dm in range(KT):
                    bo_cv = OP("vector", "cV", lambda eng, dm=dm: eng.tensor_scalar(
                        out=cxc["xf"][:, dm * 1024:(dm + 1) * 1024],
                        in0=cxc["xf"][:, dm * 1024:(dm + 1) * 1024],
                        scalar1=bo_sb[:, dm:dm + 1], scalar2=None, op0=ALU.add))
                W("sync", "cP", cxc.get("ctm_free", 0))
                ctm_dst = ap_nd(ctm_all[0:128, 0:1], 0,
                                [(8192, 128), (1024, 8), (1, 1024)])

                def ldc(eng, m=m, ctm_dst=ctm_dst):
                    idx = eng.snap(reg_sync, min_val=0, max_val=1)
                    src = cst[idx, m * 1024:m * 1024 + 128, :]
                    src.ap = bass_rust.VecI64Pair(
                        [(1024, 128), (128 * 1024, 8), (1, 1024)])
                    return eng.dma_start(out=ctm_dst, in_=src)
                cnt["dS"] = cnt.get("dS", 0) + 16
                emit("sync", lambda eng, f=ldc, s=sems["dS"]: f(eng).then_inc(s, 16))
                cld = cnt["dS"]
                emit("sync", lambda eng, s=sems["dS"], v=cld: eng.wait_ge(s, v))
                cfm_cv = 0
                for s8 in range(8):
                    par = s8 % 2
                    pT = cxc["pT"][par]
                    W("tensor", "dS", cld)
                    t1 = t2 = 0
                    cp1 = 0
                    for g in range(KT):
                        mk = (lambda eng, g=g, s8=s8, pT=pT: eng.transpose(
                            pT[:, (g % 4) * 128:(g % 4) * 128 + 128],
                            ctm_all[:, s8 * 1024 + g * 128:s8 * 1024 + g * 128 + 128],
                            ident_sb[:, :]))
                        if g == 3:
                            t1 = OP("tensor", "cP", mk)
                            W("vector", "cP", t1)
                            if cfm_free_pe:
                                W("vector", "cP", cfm_free_pe)
                                cfm_free_pe = 0
                            dst1 = ap_nd(cfm[0:128, 0:1], 0 * 1024 + s8 * 128,
                                         [(8192, 128), (1024, 4), (1, 128)])
                            cp1 = OP("vector", "cV", lambda eng, d=dst1, pT=pT: eng.tensor_copy(
                                out=d, in_=pT[:, :]))
                        elif g == KT - 1:
                            t2 = OP("tensor", "cP", mk)
                        else:
                            if g == 4:
                                W("tensor", "cV", cp1)
                            OPn("tensor", mk)
                    cxc["ctm_free"] = cnt["cP"]
                    W("vector", "cP", t2)
                    dst2 = ap_nd(cfm[0:128, 0:1], 4 * 1024 + s8 * 128,
                                 [(8192, 128), (1024, 4), (1, 128)])
                    cfm_cv = OP("vector", "cV", lambda eng, d=dst2, pT=pT: eng.tensor_copy(out=d, in_=pT[:, :]))
                    W("tensor", "cV", cfm_cv)
                W("tensor", "cV", cfm_cv)
                W("tensor", "cA", wo_ready)
                W("scalar", "dS", bold)
                for dm in range(KT):
                    for half in range(2):
                        pa_ = cxc["pA"] if half == 0 else cxc["pC"]
                        t0 = half * 512
                        amm = 0
                        for k in range(KT):
                            mk = (lambda eng, k=k, dm=dm, pa_=pa_, t0=t0: eng.matmul(
                                pa_[:, :], wo_sb[:, k * 1024 + dm * 128:k * 1024 + dm * 128 + 128],
                                cfm[:, k * 1024 + t0:k * 1024 + t0 + 512],
                                start=(k == 0), stop=(k == KT - 1)))
                            if k == KT - 1:
                                amm = OP("tensor", "cP", mk)
                            else:
                                OPn("tensor", mk)
                        W("scalar", "cP", amm)
                        aev = OP("scalar", "cA", lambda eng, dm=dm, pa_=pa_: eng.activation(
                            out=asb[:, :], in_=pa_[:, :], func=AF.Copy))
                        W("vector", "cA", aev)
                        W("vector", "cV", bo_cv)
                        x2v = OP("vector", "cV", lambda eng, dm=dm, t0=t0: eng.tensor_tensor(
                            out=cxc["xf"][:, dm * 1024 + t0:dm * 1024 + t0 + 512],
                            in0=cxc["xf"][:, dm * 1024 + t0:dm * 1024 + t0 + 512],
                            in1=asb[:, :], op=ALU.add))
                        W("tensor", "cV", x2v)
                        W("scalar", "cV", x2v)
                wo_pe = cnt["cP"]
                cfm_free_pe = wo_pe
                b0 = ln_pass(cxc, 0, src_cv=cnt["cV"])
                b1 = ln_pass(cxc, 1, src_cv=cnt["cV"])
                c0 = ln_norm(cxc, b0[0], b1[0], pe_clear=lnb_free_pe)
                W("tensor", "cV", c0[0])
                W("tensor", "cG", c0[1])
                lastcv, _, down_pe = ffn_block(cxc, None, 0.5, stash_x1=None,
                                               xf_read=(c0[0], c0[1]))
                lnb_free_pe = down_pe
                for dm in range(KT):
                    W("gpsimd", "cV", cnt["cV"])
                    out_dma = DMA("gpsimd", out[dm * 128:(dm + 1) * 128, m * 1024:(m + 1) * 1024],
                                  cxc["xf"][:, dm * 1024:(dm + 1) * 1024], "dG")
                xf_free = (cnt["cV"], cnt["cG"], out_dma, cnt["cA"])

            W("gpsimd", "dG", cnt["dG"])

        with nc.Block() as block:
            @block.sync
            def _(eng):
                for fn in sched["sync"]:
                    fn(eng)

            @block.gpsimd
            def _(eng):
                for fn in sched["gpsimd"]:
                    fn(eng)

            @block.scalar
            def _(eng):
                for fn in sched["scalar"]:
                    fn(eng)

            @block.vector
            def _(eng):
                for fn in sched["vector"]:
                    fn(eng)

            @block.tensor
            def _(eng):
                for fn in sched["tensor"]:
                    fn(eng)

    return nc


_NC_CACHE = {}


def kernel(L0_attn_x, L0_attn_W_A, L0_attn_W_V, L0_attn_W_O, L0_attn_b_O,
           L0_ffn_w_gate, L0_ffn_w_up, L0_ffn_w_down):
    x = np.ascontiguousarray(np.asarray(L0_attn_x, dtype=np.float32))
    W_A = np.asarray(L0_attn_W_A, dtype=np.float32)
    W_V = np.asarray(L0_attn_W_V, dtype=np.float32)
    W_O = np.asarray(L0_attn_W_O, dtype=np.float32)
    b_O = np.asarray(L0_attn_b_O, dtype=np.float32)
    wgi = np.asarray(L0_ffn_w_gate, dtype=np.int32)
    wui = np.asarray(L0_ffn_w_up, dtype=np.int32)
    wdi = np.asarray(L0_ffn_w_down, dtype=np.int32)

    def tile_fd(w, FT_, KT_):
        return np.ascontiguousarray(
            w.reshape(FT_, 128, KT_, 128).transpose(0, 3, 2, 1).reshape(FT_ * 128, KT_ * 128))

    wg_s = tile_fd(wgi, FT, KT)
    wu_s = tile_fd(wui, FT, KT)
    wd_s = tile_fd(wdi, KT, FT)
    wv_s = tile_fd(W_V, KT, KT).astype(np.float32)
    wo_s = tile_fd(W_O, KT, KT).astype(np.float32)
    wa_s = np.ascontiguousarray(W_A.T.astype(np.float32))
    bo_s = np.ascontiguousarray(b_O.reshape(KT, 128).T)
    ones_s = np.ones((128, 1), dtype=ml_dtypes.bfloat16)
    ident_s = np.eye(128, dtype=ml_dtypes.bfloat16)

    if "nc" not in _NC_CACHE:
        _NC_CACHE["nc"] = build()
    nc = _NC_CACHE["nc"]

    in_maps = []
    for c in range(8):
        b = c // 2
        in_maps.append(dict(
            xt=np.ascontiguousarray(x[b].T),
            wg=wg_s, wu=wu_s, wd=wd_s, wv=wv_s, wo=wo_s, wa=wa_s, bo=bo_s,
            ones=ones_s, ident=ident_s,
            cstin=np.tile(np.array([[EPS, 0.125]], np.float32), (128, 1)),
        ))
    res = run_bass_kernel_spmd(nc, in_maps, core_ids=list(range(8)))
    outp = np.empty((B, S, D), dtype=np.float32)
    for c in range(8):
        b, r = c // 2, c % 2
        outp[b, r * (S // 2):(r + 1) * (S // 2), :] = np.asarray(res.results[c]["out"]).T
    return outp



# version 07
# speedup vs baseline: 1.1662x; 1.0115x over previous
import contextlib
import numpy as np
import ml_dtypes
import bass_rust
import concourse.bass as bass
import concourse.mybir as mybir
from concourse.bass_utils import run_bass_kernel_spmd

F32 = mybir.dt.float32
BF16 = mybir.dt.bfloat16
I32 = mybir.dt.int32
AF = mybir.ActivationFunctionType
ALU = mybir.AluOpType
AX = mybir.AxisListType

B, S, D = 4, 8192, 1024
H = 16
HID = 3072
KT = D // 128
FT = HID // 128
NT = S // 128
SCALE_W = 1.0 / 127.0
EPS = 1e-5


def ap_nd(base_ap, extra_offset, dims):
    ap = base_ap
    ap.ap = bass_rust.VecI64Pair(dims)
    ap.offset = ap.offset + extra_offset
    return ap


def build():
    nc = bass.Bass("TRN2", target_bir_lowering=False)

    xt = nc.dram_tensor("xt", [D, S], F32, kind="ExternalInput").ap()
    wg = nc.dram_tensor("wg", [HID, D], I32, kind="ExternalInput").ap()
    wu = nc.dram_tensor("wu", [HID, D], I32, kind="ExternalInput").ap()
    wd = nc.dram_tensor("wd", [D, HID], I32, kind="ExternalInput").ap()
    wv = nc.dram_tensor("wv", [D, D], F32, kind="ExternalInput").ap()
    wo = nc.dram_tensor("wo", [D, D], F32, kind="ExternalInput").ap()
    wa = nc.dram_tensor("wa", [D, H], F32, kind="ExternalInput").ap()
    bo = nc.dram_tensor("bo", [128, KT], F32, kind="ExternalInput").ap()
    ones = nc.dram_tensor("ones", [128, 1], BF16, kind="ExternalInput").ap()
    ident = nc.dram_tensor("ident", [128, 128], BF16, kind="ExternalInput").ap()
    cstin = nc.dram_tensor("cstin", [128, 2], F32, kind="ExternalInput").ap()
    out = nc.dram_tensor("out", [D, S // 2], F32, kind="ExternalOutput").ap()

    wsg = nc.dram_tensor("wsg", [HID, D], BF16)
    wsu = nc.dram_tensor("wsu", [HID, D], BF16)
    wsd = nc.dram_tensor("wsd", [D, HID], BF16)
    vst = nc.dram_tensor("vst", [S, D], BF16)
    x1st = nc.dram_tensor("x1st", [2, D, S // 2], F32)
    cst = nc.dram_tensor("cst", [2, S // 2, D], BF16)
    zst = nc.dram_tensor("zst", [H, S], F32)

    sched = {"sync": [], "gpsimd": [], "scalar": [], "vector": [], "tensor": []}
    cnt = {}

    def emit(engname, fn):
        sched[engname].append(fn)

    with contextlib.ExitStack() as top:
        E = top.enter_context
        sems = {s: E(nc.semaphore(s)) for s in ["dS", "dG", "dX", "cP", "cA", "cV", "cG"]}

        def W(engname, sem_name, val):
            if val <= 0:
                return
            s = sems[sem_name]
            emit(engname, lambda eng, s=s, v=val: eng.wait_ge(s, v))

        def DMA(engname, dst, src, sem_name):
            cnt[sem_name] = cnt.get(sem_name, 0) + 16
            s = sems[sem_name]
            emit(engname, lambda eng, d=dst, so=src, s=s: eng.dma_start(out=d, in_=so).then_inc(s, 16))
            v = cnt[sem_name]
            emit(engname, lambda eng, s=s, v=v: eng.wait_ge(s, v))
            return cnt[sem_name]

        def DMAn(engname, dst, src, sem_name):
            cnt[sem_name] = cnt.get(sem_name, 0) + 16
            s = sems[sem_name]
            emit(engname, lambda eng, d=dst, so=src, s=s: eng.dma_start(out=d, in_=so).then_inc(s, 16))
            return cnt[sem_name]

        def Wself(engname, sem_name):
            v = cnt.get(sem_name, 0)
            s = sems[sem_name]
            emit(engname, lambda eng, s=s, v=v: eng.wait_ge(s, v))
            return v

        def OP(engname, sem_name, fn):
            cnt[sem_name] = cnt.get(sem_name, 0) + 1
            s = sems[sem_name]
            emit(engname, lambda eng, f=fn, s=s: f(eng).then_inc(s, 1))
            return cnt[sem_name]

        def OPn(engname, fn):
            emit(engname, lambda eng, f=fn: f(eng))

        ident_sb = E(nc.sbuf_tensor("ident_sb", [128, 128], BF16))
        ones_sb = E(nc.sbuf_tensor("ones_sb", [128, 1], BF16))
        csts = E(nc.sbuf_tensor("csts", [128, 2], F32))
        reg_sync = E(nc.sync.register("ridx"))
        reg_holder = {}

        DMA("sync", ident_sb[:, :], ident[:, :], "dS")
        DMA("sync", ones_sb[:, :], ones[:, :], "dS")
        DMA("sync", csts[:, :], cstin[:, :], "dS")

        with contextlib.ExitStack() as p0:
            wis = [p0.enter_context(nc.sbuf_tensor(f"wi{i}", [128, 3072], I32)) for i in range(2)]
            wbs = [p0.enter_context(nc.sbuf_tensor(f"wb{i}", [128, 3072], BF16)) for i in range(2)]
            chunks = []
            for f in range(FT):
                chunks.append((wg, wsg, f * 128, 1024))
                chunks.append((wu, wsu, f * 128, 1024))
            for k in range(KT):
                chunks.append((wd, wsd, k * 128, 3072))
            last_st = {0: 0, 1: 0}
            last_cv = {0: 0, 1: 0}
            for i, (srcd, dstd, r0, ncol) in enumerate(chunks):
                par = i % 2
                qeng, qsem = ("sync", "dS") if par == 0 else ("scalar", "dX")
                W(qeng, "cV", last_cv[par])
                ld = DMAn(qeng, wis[par][:, 0:ncol], srcd[r0:r0 + 128, 0:ncol], qsem)
                W("vector", qsem, ld)
                W("vector", "dG", last_st[par])
                last_cv[par] = OP("vector", "cV", lambda eng, p=par, n=ncol: eng.tensor_scalar(
                    out=wbs[p][:, 0:n], in0=wis[p][:, 0:n],
                    scalar1=SCALE_W, scalar2=None, op0=ALU.mult))
                W("gpsimd", "cV", last_cv[par])
                last_st[par] = DMAn("gpsimd", dstd.ap()[r0:r0 + 128, 0:ncol], wbs[par][:, 0:ncol], "dG")
            w_stash_done = cnt["dG"]

        def make_ffn_ctx(Ec, tag):
            ctx = {}
            ctx["xf"] = Ec(nc.sbuf_tensor("xf" + tag, [128, 8192], F32))
            ctx["lnb"] = Ec(nc.sbuf_tensor("lnb" + tag, [128, 8192], BF16))
            ctx["h_sb"] = Ec(nc.sbuf_tensor("h" + tag, [128, FT * 1024], BF16))
            ctx["sg"] = Ec(nc.sbuf_tensor("sg" + tag, [128, 512], BF16))
            ctx["xb"] = Ec(nc.sbuf_tensor("xb" + tag, [128, 1024], BF16))
            ctx["sqb"] = Ec(nc.sbuf_tensor("sqb" + tag, [128, 1024], BF16))
            ctx["tmpn"] = Ec(nc.sbuf_tensor("tmpn" + tag, [128, 512], F32))
            ctx["tmpg"] = Ec(nc.sbuf_tensor("tmpg" + tag, [128, 512], F32))
            ctx["st"] = Ec(nc.sbuf_tensor("st" + tag, [1, 5120], F32))
            ctx["bcb"] = Ec(nc.sbuf_tensor("bcb" + tag, [128, 2048], F32))
            ctx["wgus"] = [Ec(nc.sbuf_tensor(f"wgu{i}" + tag, [128, 2048], BF16)) for i in range(2)]
            ctx["wdbs"] = [Ec(nc.sbuf_tensor(f"wdb{i}" + tag, [128, 3072], BF16)) for i in range(2)]
            ctx["pA"] = Ec(nc.psum_tensor("pA" + tag, [128, 512], F32))
            ctx["pB"] = Ec(nc.psum_tensor("pB" + tag, [128, 512], F32))
            ctx["pC"] = Ec(nc.psum_tensor("pC" + tag, [128, 512], F32))
            ctx["pD"] = Ec(nc.psum_tensor("pD" + tag, [128, 512], F32))
            ctx["pT"] = [Ec(nc.psum_tensor(f"pT{i}" + tag, [128, 512], BF16)) for i in range(2)]
            return ctx

        def ln_pass(cx, half, src_cv=0):
            xf = cx["xf"]
            ev = "vector" if half == 0 else "gpsimd"
            sv = "cV" if half == 0 else "cG"
            so = half * 512
            sto = half * 2560
            xb = cx["xb"][:, so:so + 512]
            sqb = cx["sqb"][:, so:so + 512]
            st_ = cx["st"]
            bcb = cx["bcb"]
            class _S:
                def __getitem__(self, idx):
                    a, b = idx[1].start, idx[1].stop
                    return st_[0:1, sto + a:sto + b]
            st = _S()
            pA = cx["pA"] if half == 0 else cx["pC"]
            pB = cx["pB"] if half == 0 else cx["pD"]
            t0 = half * 512
            if src_cv:
                W(ev, "cV", src_cv)
                W("scalar", "cV", src_cv)
            W("tensor", "cA", cnt.get("cA", 0))
            for k in range(KT):
                sl = slice(k * 1024 + t0, k * 1024 + t0 + 512)
                cv = OP(ev, sv, lambda eng, sl=sl: eng.tensor_copy(
                    out=xb[:, :], in_=xf[:, sl]))
                ca = OP("scalar", "cA", lambda eng, sl=sl: eng.activation(
                    out=sqb[:, :], in_=xf[:, sl], func=AF.Square))
                W("tensor", sv, cv)
                W("tensor", "cA", ca)
                v1 = OP("tensor", "cP", lambda eng, k=k: eng.matmul(
                    pA[0:1, :], ones_sb[:, :], xb[:, :], start=(k == 0), stop=(k == KT - 1)))
                v2 = OP("tensor", "cP", lambda eng, k=k: eng.matmul(
                    pB[0:1, :], ones_sb[:, :], sqb[:, :], start=(k == 0), stop=(k == KT - 1)))
                if k < KT - 1:
                    W(ev, "cP", v1)
                    W("scalar", "cP", v2)
            W("vector", "cP", cnt["cP"])
            W("vector", "cV", cx.get(("prev_mr", half), 0))
            v_ = OP("vector", "cV", lambda eng: eng.tensor_scalar(
                out=st[0:1, 0:512], in0=pA[0:1, :], scalar1=1.0 / D, scalar2=None, op0=ALU.mult))
            pa_free = OP("vector", "cV", lambda eng: eng.tensor_scalar(
                out=st[0:1, 1536:2048], in0=pB[0:1, :], scalar1=1.0 / D, scalar2=None, op0=ALU.mult))
            W(ev, "cV", v_)
            v_ = OP(ev, sv, lambda eng: eng.tensor_tensor(
                out=st[0:1, 2048:2560], in0=st[0:1, 0:512], in1=st[0:1, 0:512], op=ALU.mult))
            W(ev, sv, v_)
            W(ev, "cV", pa_free)
            cv = OP(ev, sv, lambda eng: eng.tensor_tensor(
                out=st[0:1, 1536:2048], in0=st[0:1, 1536:2048], in1=st[0:1, 2048:2560], op=ALU.subtract))
            W("scalar", sv, cv)
            ca = OP("scalar", "cA", lambda eng: eng.activation(
                out=st[0:1, 2048:2560], in_=st[0:1, 1536:2048], func=AF.Sqrt, bias=csts[0:1, 0:1]))
            W("vector", "cA", ca)
            W("vector", "cV", cx.get(("prev_mr", half), 0))
            v_ = OP("vector", "cV", lambda eng: eng.reciprocal(
                out=st[0:1, 512:1024], in_=st[0:1, 2048:2560]))
            W("vector", "cV", v_)
            cv = OP("vector", "cV", lambda eng: eng.tensor_tensor(
                out=st[0:1, 1024:1536], in0=st[0:1, 0:512], in1=st[0:1, 512:1024], op=ALU.mult))
            cx[("prev_mr", half)] = cv
            W("gpsimd", "cV", cv)
            W("gpsimd", sv, cx.get(("bc_free", half), 0))
            bsrc1 = ap_nd(st[0:1, 0:1], 512, [(5120, 1), (0, 128), (1, 512)])
            DMAn("gpsimd", bcb[:, half * 512:half * 512 + 512], bsrc1, "dG")
            bsrc2 = ap_nd(st[0:1, 0:1], 1024, [(5120, 1), (0, 128), (1, 512)])
            bc = DMAn("gpsimd", bcb[:, 1024 + half * 512:1024 + half * 512 + 512], bsrc2, "dG")
            Wself("gpsimd", "dG")
            return bc, pa_free

        def ln_norm(cx, bc0, bc1, pe_clear=0):
            xf, lnb, bcb = cx["xf"], cx["lnb"], cx["bcb"]
            lastcv = cx.get("tmp_cv", 0)
            lastcg = cx.get("tmp_cg", 0)
            W("vector", "dG", max(bc0, bc1))
            W("gpsimd", "dG", max(bc0, bc1))
            if pe_clear:
                W("vector", "cP", pe_clear)
                W("gpsimd", "cP", pe_clear)
            for k in range(KT):
                sl = slice(k * 1024, (k + 1) * 1024)
                ev = "vector" if k % 2 == 0 else "gpsimd"
                sv = "cV" if k % 2 == 0 else "cG"
                v_ = OP(ev, sv, lambda eng, sl=sl: eng.tensor_tensor(
                    out=lnb[:, sl], in0=xf[:, sl], in1=bcb[:, 0:1024], op=ALU.mult))
                W(ev, sv, v_)
                m_ = OP(ev, sv, lambda eng, sl=sl: eng.tensor_tensor(
                    out=lnb[:, sl], in0=lnb[:, sl], in1=bcb[:, 1024:2048], op=ALU.subtract))
                if k % 2 == 0:
                    lastcv = m_
                else:
                    lastcg = m_
            cx["tmp_cv"] = lastcv
            cx["tmp_cg"] = lastcg
            cx[("bc_free", 0)] = lastcv
            cx[("bc_free", 1)] = lastcg
            return lastcv, lastcg

        def ffn_block(cx, wload_gate, half_scale, stash_x1=None, xf_read=(0, 0)):
            lnb, h_sb, sg, xf = cx["lnb"], cx["h_sb"], cx["sg"], cx["xf"]
            wgus, wdbs = cx["wgus"], cx["wdbs"]
            pA, pB, pC, pD = cx["pA"], cx["pB"], cx["pC"], cx["pD"]
            prev_sgv = {}
            prev_hd = {}
            for f in range(FT):
                par = f % 2
                W("sync", "cP", cx.get("wgu_free", {}).get(par, 0))
                DMAn("sync", wgus[par][:, 0:1024], wsg.ap()[f * 128:(f + 1) * 128, :], "dS")
                l2 = DMAn("sync", wgus[par][:, 1024:2048], wsu.ap()[f * 128:(f + 1) * 128, :], "dS")
                Wself("sync", "dS")
                W("tensor", "dS", l2)
                for half in range(2):
                    pg = pA if half == 0 else pC
                    pu = pB if half == 0 else pD
                    t0 = half * 512
                    if (half,) in prev_sgv:
                        W("tensor", "cA", prev_sgv[(half,)])
                        W("tensor", "cV", prev_hd[(half,)])
                    gate_done = up_done = 0
                    for k in range(KT):
                        mk = (lambda eng, k=k, par=par, pg=pg, t0=t0: eng.matmul(
                            pg[:, :], wgus[par][:, k * 128:(k + 1) * 128],
                            lnb[:, k * 1024 + t0:k * 1024 + t0 + 512],
                            start=(k == 0), stop=(k == KT - 1)))
                        if k == KT - 1:
                            gate_done = OP("tensor", "cP", mk)
                        else:
                            OPn("tensor", mk)
                    for k in range(KT):
                        mk = (lambda eng, k=k, par=par, pu=pu, t0=t0: eng.matmul(
                            pu[:, :], wgus[par][:, 1024 + k * 128:1024 + (k + 1) * 128],
                            lnb[:, k * 1024 + t0:k * 1024 + t0 + 512],
                            start=(k == 0), stop=(k == KT - 1)))
                        if k == KT - 1:
                            up_done = OP("tensor", "cP", mk)
                        else:
                            OPn("tensor", mk)
                    W("scalar", "cP", gate_done)
                    W("scalar", "cV", cx.get("prev_hdone", 0))
                    sgv = OP("scalar", "cA", lambda eng, pg=pg: eng.activation(
                        out=sg[:, :], in_=pg[:, :], func=AF.Silu))
                    W("vector", "cP", up_done)
                    W("vector", "cA", sgv)
                    hdone = OP("vector", "cV", lambda eng, f=f, half=half, pu=pu: eng.tensor_tensor(
                        out=h_sb[:, f * 1024 + half * 512:f * 1024 + half * 512 + 512],
                        in0=sg[:, :], in1=pu[:, :], op=ALU.mult))
                    cx["prev_hdone"] = hdone
                    prev_sgv[(half,)] = sgv
                    prev_hd[(half,)] = hdone
                cx.setdefault("wgu_free", {})[par] = cnt["cP"]
            h_all = cnt["cV"]
            last_dma = 0
            for dm in range(KT):
                par = dm % 2
                W("sync", "cP", cx.get("wdb_free", {}).get(par, 0))
                l = DMA("sync", wdbs[par][:, :], wsd.ap()[dm * 128:(dm + 1) * 128, :], "dS")
                W("tensor", "dS", l)
                W("tensor", "cV", h_all)
                for half in range(2):
                    pd = pA if half == 0 else pC
                    t0 = half * 512
                    dn = 0
                    if dm == 0 and half == 0:
                        W("vector", "cV", xf_read[0])
                        W("vector", "cG", xf_read[1])
                    if dm == 0:
                        W("tensor", "cA", prev_sgv[(half,)])
                    else:
                        W("tensor", "cV", cx.get("prev_x1v", {}).get(half, 0))
                    for f in range(FT):
                        mk = (lambda eng, f=f, par=par, pd=pd, half=half: eng.matmul(
                            pd[:, :], wdbs[par][:, f * 128:(f + 1) * 128],
                            h_sb[:, f * 1024 + half * 512:f * 1024 + half * 512 + 512],
                            start=(f == 0), stop=(f == FT - 1)))
                        if f == FT - 1:
                            dn = OP("tensor", "cP", mk)
                        else:
                            OPn("tensor", mk)
                    W("vector", "cP", dn)
                    x1v = OP("vector", "cV", lambda eng, dm=dm, t0=t0, pd=pd, hs=half_scale: eng.scalar_tensor_tensor(
                        out=xf[:, dm * 1024 + t0:dm * 1024 + t0 + 512],
                        in0=pd[:, :], scalar=hs,
                        in1=xf[:, dm * 1024 + t0:dm * 1024 + t0 + 512],
                        op0=ALU.mult, op1=ALU.add))
                    cx.setdefault("prev_x1v", {})[half] = x1v
                cx.setdefault("wdb_free", {})[par] = cnt["cP"]
                if stash_x1 is not None:
                    W("gpsimd", "cV", cnt["cV"])
                    last_dma = DMAn("gpsimd", stash_x1(dm), xf[:, dm * 1024:(dm + 1) * 1024], "dG")
            if stash_x1 is not None:
                last_dma = Wself("gpsimd", "dG")
            return cnt["cV"], last_dma, cnt["cP"]

        NMACRO = S // 1024
        HFM = max(NMACRO // 2, 1)
        NCH = NT // 8
        with contextlib.ExitStack() as pa_stack:
            Ea = pa_stack.enter_context
            cx = make_ffn_ctx(Ea, "a")
            vf = Ea(nc.sbuf_tensor("vf", [128, 4096], BF16))
            vts = [Ea(nc.sbuf_tensor(f"vt{i}", [128, 1024], BF16)) for i in range(2)]
            wv_sb = Ea(nc.sbuf_tensor("wv_sb", [128, KT * 1024], BF16))
            zstg = Ea(nc.sbuf_tensor("zstg", [H, 512], F32))
            wa_sb = Ea(nc.sbuf_tensor("wa_sb", [128, KT * H], BF16))
            wvt = Ea(nc.sbuf_tensor("wvt", [128, 1024], F32))

            W("sync", "cV", cnt["cV"])
            W("sync", "dG", w_stash_done)
            prev_ca = 0
            for k in range(KT):
                W("sync", "cA", prev_ca)
                ld = DMA("sync", wvt[:, :], wv[k * 128:(k + 1) * 128, :], "dS")
                W("scalar", "dS", ld)
                prev_ca = OP("scalar", "cA", lambda eng, k=k: eng.activation(
                    out=wv_sb[:, k * 1024:(k + 1) * 1024], in_=wvt[:, :], func=AF.Copy))
            wa_src = wa.copy()
            wa_src.ap = bass_rust.VecI64Pair([(H, 128), (128 * H, KT), (1, H)])
            wa_src.offset = 0
            W("sync", "cA", prev_ca)
            wadst = ap_nd(wvt[0:128, 0:1], 0, [(1024, 128), (H, KT), (1, H)])
            ld = DMA("sync", wadst, wa_src, "dS")
            W("scalar", "dS", ld)
            wa_ready = OP("scalar", "cA", lambda eng: eng.activation(
                out=wa_sb[:, :], in_=wvt[:, 0:KT * H], func=AF.Copy))

            vz_pe_done = 0
            lnb_free_pe = 0
            xf_free = (0, 0, 0, 0)
            vt_dma = {0: 0, 1: 0}
            for m in range(NMACRO):
                W("sync", "cV", xf_free[0])
                W("sync", "cG", xf_free[1])
                W("sync", "dG", xf_free[2])
                W("sync", "cA", xf_free[3])
                if m == 0:
                    W("sync", "dG", w_stash_done)
                xld = 0
                for k in range(KT):
                    xld = DMAn("sync", cx["xf"][:, k * 1024:(k + 1) * 1024],
                               xt[k * 128:(k + 1) * 128, m * 1024:(m + 1) * 1024], "dS")
                Wself("sync", "dS")
                for e in ["vector", "scalar", "tensor", "gpsimd"]:
                    W(e, "dS", xld)
                b0 = ln_pass(cx, 0)
                b1 = ln_pass(cx, 1)
                c0 = ln_norm(cx, b0[0], b1[0], pe_clear=lnb_free_pe)
                W("tensor", "cV", c0[0])
                W("tensor", "cG", c0[1])
                hf = m // HFM

                def stash(dm, m=m, hf=hf):
                    return x1st.ap()[hf, dm * 128:(dm + 1) * 128,
                                     (m % HFM) * 1024:(m % HFM) * 1024 + 1024]
                lastcv, x1dma, down_pe = ffn_block(cx, None, 0.5, stash_x1=stash,
                                                   xf_read=(c0[0], c0[1]))
                b0 = ln_pass(cx, 0, src_cv=lastcv)
                b1 = ln_pass(cx, 1, src_cv=lastcv)
                c0 = ln_norm(cx, b0[0], b1[0], pe_clear=down_pe)
                W("tensor", "cV", c0[0])
                W("tensor", "cG", c0[1])
                W("tensor", "cA", wa_ready)
                lnbA = cx["lnb"]
                pDA = cx["pD"]
                for half in range(2):
                    t0 = half * 512
                    vev_slot = cx.setdefault("vev_slot", {})
                    for fv in range(KT):
                        pv = [cx["pA"], cx["pC"], cx["pB"], cx["pD"]][fv % 4]
                        if (fv % 4) in vev_slot:
                            W("tensor", "cA", vev_slot[fv % 4])
                        elif fv == 0:
                            W("tensor", "cV", cx.get("prev_x1v", {}).get(1, 0))
                        vmm = 0
                        for k in range(KT):
                            mk = (lambda eng, fv=fv, k=k, pv=pv, t0=t0: eng.matmul(
                                pv[:, :], wv_sb[:, k * 1024 + fv * 128:k * 1024 + fv * 128 + 128],
                                lnbA[:, k * 1024 + t0:k * 1024 + t0 + 512],
                                start=(k == 0), stop=(k == KT - 1)))
                            if k == KT - 1:
                                vmm = OP("tensor", "cP", mk)
                            else:
                                OPn("tensor", mk)
                        W("scalar", "cP", vmm)
                        vev = OP("scalar", "cA", lambda eng, fv=fv, pv=pv: eng.activation(
                            out=vf[:, fv * 512:(fv + 1) * 512], in_=pv[:, :], func=AF.Copy))
                        vev_slot[fv % 4] = vev
                    W("tensor", "cA", cx["vev_slot"][3])
                    zmm = 0
                    for k in range(KT):
                        mk = (lambda eng, k=k, t0=t0: eng.matmul(
                            pDA[0:16, :], wa_sb[:, k * H:(k + 1) * H],
                            lnbA[:, k * 1024 + t0:k * 1024 + t0 + 512],
                            start=(k == 0), stop=(k == KT - 1)))
                        if k == KT - 1:
                            zmm = OP("tensor", "cP", mk)
                        else:
                            OPn("tensor", mk)
                    W("scalar", "cP", zmm)
                    W("scalar", "dG", cx.get("zdma", 0))
                    zev = OP("scalar", "cA", lambda eng: eng.activation(
                        out=zstg[:, :], in_=pDA[0:16, :], func=AF.Copy,
                        scale=csts[0:16, 1:2]))
                    W("tensor", "cA", zev)
                    W("gpsimd", "cA", zev)
                    cx["zdma"] = DMA("gpsimd", zst.ap()[0:H, m * 1024 + t0:m * 1024 + t0 + 512],
                                     zstg[:, :], "dG")
                    for s4 in range(4):
                        par = s4 % 2
                        vt = vts[par]
                        pT = cx["pT"][par]
                        t1 = t2 = 0
                        cp1 = 0
                        for fv in range(KT):
                            mk = (lambda eng, fv=fv, s4=s4, pT=pT: eng.transpose(
                                pT[:, (fv % 4) * 128:(fv % 4) * 128 + 128],
                                vf[:, fv * 512 + s4 * 128:fv * 512 + s4 * 128 + 128],
                                ident_sb[:, :]))
                            if fv == 3:
                                t1 = OP("tensor", "cP", mk)
                                W("vector", "cP", t1)
                                W("vector", "dG", vt_dma[par])
                                cp1 = OP("vector", "cV", lambda eng, vt=vt, pT=pT: eng.tensor_copy(
                                    out=vt[:, 0:512], in_=pT[:, :]))
                            elif fv == KT - 1:
                                t2 = OP("tensor", "cP", mk)
                            else:
                                if fv == 4:
                                    W("tensor", "cV", cp1)
                                OPn("tensor", mk)
                        W("vector", "cP", t2)
                        cvv = OP("vector", "cV", lambda eng, vt=vt, pT=pT: eng.tensor_copy(
                            out=vt[:, 512:1024], in_=pT[:, :]))
                        W("gpsimd", "cV", cvv)
                        row0 = (m * 8 + half * 4 + s4) * 128
                        vt_dma[par] = DMA("gpsimd", vst.ap()[row0:row0 + 128, :], vt[:, :], "dG")
                        W("tensor", "cV", cvv)
                vz_pe_done = cnt["cP"]
                lnb_free_pe = vz_pe_done
                xf_free = (cnt["cV"], cnt["cG"], x1dma, cnt["cA"])
            a_end = {k: cnt.get(k, 0) for k in ["cP", "cA", "cV", "cG", "dS", "dG"]}

        zpb_stack = contextlib.ExitStack()
        zpb = zpb_stack.enter_context(nc.sbuf_tensor("zpb", [H, S + 256], BF16))
        with contextlib.ExitStack() as pbs:
            red = pbs.enter_context(nc.sbuf_tensor("red", [H, 2], F32))
            z_sb = pbs.enter_context(nc.sbuf_tensor("z_sb", [H, S], F32))
            W("sync", "dG", a_end["dG"])
            W("sync", "cP", a_end["cP"])
            W("sync", "cA", a_end["cA"])
            W("sync", "cV", a_end["cV"])
            W("sync", "cG", a_end["cG"])
            zld = DMA("sync", z_sb[:, :], zst.ap()[:, :], "dS")
            W("vector", "dS", zld)
            W("vector", "cA", a_end["cA"])
            rv = OP("vector", "cV", lambda eng: eng.reduce_max(
                out=red[:, 0:1], in_=z_sb[:, :], axis=AX.X, negate=True))
            W("scalar", "cV", rv)
            ev = OP("scalar", "cA", lambda eng: eng.activation(
                out=z_sb[:, :], in_=z_sb[:, :], func=AF.Exp, bias=red[:, 0:1]))
            W("vector", "cA", ev)
            v_ = OP("vector", "cV", lambda eng: eng.reduce_sum(
                out=red[:, 1:2], in_=z_sb[:, :], axis=AX.X))
            W("vector", "cV", v_)
            v_ = OP("vector", "cV", lambda eng: eng.reciprocal(out=red[:, 1:2], in_=red[:, 1:2]))
            W("vector", "cV", v_)
            v_ = OP("vector", "cV", lambda eng: eng.memset(zpb[:, :], 0.0))
            W("vector", "cV", v_)
            zpv = OP("vector", "cV", lambda eng: eng.tensor_scalar(
                out=zpb[:, 128:128 + S], in0=z_sb[:, :],
                scalar1=red[:, 1:2], scalar2=None, op0=ALU.mult))

        with contextlib.ExitStack() as pb_stack:
            Eb = pb_stack.enter_context
            v_sb = Eb(nc.sbuf_tensor("v_sb", [128, NT * 1024], BF16))
            Ts = [Eb(nc.sbuf_tensor(f"T{i}", [128, S], BF16)) for i in range(2)]
            conv_q = Eb(nc.sbuf_tensor("conv_q", [128, NCH * 512], BF16))
            pp = [Eb(nc.psum_tensor(f"pp{i}", [128, 512], F32)) for i in range(4)]

            W("sync", "cV", zpv)
            W("scalar", "cV", zpv)
            vld = vldx = 0
            for J in range(NT):
                if J % 2 == 0:
                    vld = DMAn("sync", v_sb[:, J * 1024:(J + 1) * 1024],
                               vst.ap()[J * 128:(J + 1) * 128, :], "dS")
                else:
                    vldx = DMAn("scalar", v_sb[:, J * 1024:(J + 1) * 1024],
                                vst.ap()[J * 128:(J + 1) * 128, :], "dX")
                if J % 8 == 7:
                    Wself("sync", "dS")
                    Wself("scalar", "dX")

            def build_T(h, par):
                src = ap_nd(zpb[h:h + 1, 0:1], 128, [(S + 256, 1), (-1, 128), (1, S)])
                return DMA("gpsimd", Ts[par][:, :], src, "dG")

            W("gpsimd", "cV", zpv)
            tb = {0: build_T(0, 0), 1: build_T(1, 1)}
            W("tensor", "dS", vld)
            W("tensor", "dX", vldx)
            W("tensor", "cA", a_end["cA"])
            W("tensor", "cV", a_end["cV"])
            evac_mark = {}
            stash_mark = {}
            for h in range(H):
                par = h % 2
                W("tensor", "dG", tb[h])
                for c in range(NCH):
                    ps = pp[c % 4]
                    prevkey = (h, c - 4) if c >= 4 else (h - 1, c + 4 * ((NCH - 1 - c) // 4))
                    if prevkey in evac_mark:
                        sn, vv = evac_mark[prevkey]
                        W("tensor", sn, vv)
                    mm = 0
                    for d in range(0, 8 * c + 8):
                        if d <= 8 * c:
                            nblk, j0, col0 = 8, 8 * c - d, 0
                        else:
                            nblk, j0, col0 = 8 - (d - 8 * c), 0, (d - 8 * c) * 64
                        rhs = ap_nd(v_sb[0:128, 0:1], j0 * 1024 + h * 64,
                                    [(NT * 1024, 128), (1024, nblk), (1, 64)])
                        mk = (lambda eng, par=par, d=d, ps=ps, rhs=rhs, col0=col0, nblk=nblk, c=c: eng.matmul(
                            ps[:, col0:col0 + nblk * 64],
                            Ts[par][:, d * 128:(d + 1) * 128], rhs,
                            start=(d == 0), stop=(d == 8 * c + 7)))
                        if d == 8 * c + 7:
                            mm = OP("tensor", "cP", mk)
                        else:
                            OPn("tensor", mk)
                    engn = "vector" if c % 2 == 0 else "scalar"
                    sn = "cV" if engn == "vector" else "cA"
                    W(engn, "cP", mm)
                    if (h - 1, c) in stash_mark:
                        W(engn, "dG", stash_mark[(h - 1, c)])
                    if engn == "vector":
                        OP(engn, sn, lambda eng, c=c, ps=ps: eng.tensor_copy(
                            out=conv_q[:, c * 512:(c + 1) * 512], in_=ps[:, :]))
                    else:
                        OP(engn, sn, lambda eng, c=c, ps=ps: eng.activation(
                            out=conv_q[:, c * 512:(c + 1) * 512], in_=ps[:, :], func=AF.Copy))
                    evac_mark[(h, c)] = (sn, cnt[sn])
                    W("gpsimd", sn, cnt[sn])
                    row_g = c * 8 * 128
                    half = row_g // (S // 2)
                    rowin = row_g % (S // 2)
                    dst = cst.ap().copy()
                    dst = ap_nd(dst, half * (S // 2) * D + rowin * D + h * 64,
                                [(1024, 128), (128 * 1024, 8), (1, 64)])
                    csrc = ap_nd(conv_q[0:128, 0:1], c * 512, [(NCH * 512, 128), (64, 8), (1, 64)])
                    DMAn("gpsimd", dst, csrc, "dG")
                    if c == NCH - 1:
                        htot = Wself("gpsimd", "dG")
                        for cc in range(NCH):
                            stash_mark[(h, cc)] = htot
                if h + 2 < H:
                    pe_done = cnt["cP"]
                    W("gpsimd", "cP", pe_done)
                    tb[h + 2] = build_T(h + 2, par)
        zpb_stack.close()
        b_end = {k: cnt.get(k, 0) for k in ["cP", "cA", "cV", "cG", "dS", "dG"]}

        with contextlib.ExitStack() as pc_stack:
            Ec = pc_stack.enter_context
            cxc = make_ffn_ctx(Ec, "c")
            cfm = Ec(nc.sbuf_tensor("cfm", [128, 8192], BF16))
            ctm_all = Ec(nc.sbuf_tensor("ctm_all", [128, 8192], BF16))
            asb = Ec(nc.sbuf_tensor("asb", [128, 512], F32))
            wo_sb = Ec(nc.sbuf_tensor("wo_sb", [128, KT * 1024], BF16))
            bo_sb = Ec(nc.sbuf_tensor("bo_sb", [128, KT], F32))

            for snm in ["cP", "cA", "cV", "cG", "dG"]:
                W("sync", snm, b_end[snm])
            with nc.sbuf_tensor("wvtc", [128, 1024], F32) as wvt2:
                prev_ca = 0
                for k in range(KT):
                    W("sync", "cA", prev_ca)
                    ld = DMA("sync", wvt2[:, :], wo[k * 128:(k + 1) * 128, :], "dS")
                    W("scalar", "dS", ld)
                    prev_ca = OP("scalar", "cA", lambda eng, k=k: eng.activation(
                        out=wo_sb[:, k * 1024:(k + 1) * 1024], in_=wvt2[:, :], func=AF.Copy))
            wo_ready = prev_ca
            bold = DMA("sync", bo_sb[:, :], bo[:, :], "dS")

            def sync_reg_init(eng):
                pid = eng.partition_id()
                eng.reg_mov(reg_sync, pid)
                eng.reg_mod(reg_sync, reg_sync, 2)
            emit("sync", sync_reg_init)

            xf_free = (0, 0, 0, 0)
            cfm_free_pe = 0
            lnb_free_pe = 0
            out_dma = 0
            for m in range((S // 2) // 1024):
                W("sync", "cV", xf_free[0])
                W("sync", "cG", xf_free[1])
                W("sync", "dG", xf_free[2])
                W("sync", "cA", xf_free[3])
                dst_x1 = ap_nd(cxc["xf"][0:128, 0:1], 0,
                               [(8192, 128), (1024, KT), (1, 1024)])

                def ldx1(eng, m=m, dst_ap=dst_x1):
                    idx = eng.snap(reg_sync, min_val=0, max_val=1)
                    src = x1st[idx, 0:128, m * 1024:(m + 1) * 1024]
                    src.ap = bass_rust.VecI64Pair(
                        [(S // 2, 128), (128 * (S // 2), KT), (1, 1024)])
                    return eng.dma_start(out=dst_ap, in_=src)
                cnt["dS"] = cnt.get("dS", 0) + 16
                emit("sync", lambda eng, f=ldx1, s=sems["dS"]: f(eng).then_inc(s, 16))
                x1ld = cnt["dS"]
                emit("sync", lambda eng, s=sems["dS"], v=x1ld: eng.wait_ge(s, v))
                for e in ["vector", "scalar", "tensor", "gpsimd"]:
                    W(e, "dS", x1ld)
                W("vector", "dS", bold)
                bo_cv = 0
                for dm in range(KT):
                    bo_cv = OP("vector", "cV", lambda eng, dm=dm: eng.tensor_scalar(
                        out=cxc["xf"][:, dm * 1024:(dm + 1) * 1024],
                        in0=cxc["xf"][:, dm * 1024:(dm + 1) * 1024],
                        scalar1=bo_sb[:, dm:dm + 1], scalar2=None, op0=ALU.add))
                W("sync", "cP", cxc.get("ctm_free", 0))
                ctm_dst = ap_nd(ctm_all[0:128, 0:1], 0,
                                [(8192, 128), (1024, 8), (1, 1024)])

                def ldc(eng, m=m, ctm_dst=ctm_dst):
                    idx = eng.snap(reg_sync, min_val=0, max_val=1)
                    src = cst[idx, m * 1024:m * 1024 + 128, :]
                    src.ap = bass_rust.VecI64Pair(
                        [(1024, 128), (128 * 1024, 8), (1, 1024)])
                    return eng.dma_start(out=ctm_dst, in_=src)
                cnt["dS"] = cnt.get("dS", 0) + 16
                emit("sync", lambda eng, f=ldc, s=sems["dS"]: f(eng).then_inc(s, 16))
                cld = cnt["dS"]
                emit("sync", lambda eng, s=sems["dS"], v=cld: eng.wait_ge(s, v))
                cfm_cv = 0
                for s8 in range(8):
                    par = s8 % 2
                    pT = cxc["pT"][par]
                    W("tensor", "dS", cld)
                    t1 = t2 = 0
                    cp1 = 0
                    for g in range(KT):
                        mk = (lambda eng, g=g, s8=s8, pT=pT: eng.transpose(
                            pT[:, (g % 4) * 128:(g % 4) * 128 + 128],
                            ctm_all[:, s8 * 1024 + g * 128:s8 * 1024 + g * 128 + 128],
                            ident_sb[:, :]))
                        if g == 3:
                            t1 = OP("tensor", "cP", mk)
                            W("vector", "cP", t1)
                            if cfm_free_pe:
                                W("vector", "cP", cfm_free_pe)
                                cfm_free_pe = 0
                            dst1 = ap_nd(cfm[0:128, 0:1], 0 * 1024 + s8 * 128,
                                         [(8192, 128), (1024, 4), (1, 128)])
                            cp1 = OP("vector", "cV", lambda eng, d=dst1, pT=pT: eng.tensor_copy(
                                out=d, in_=pT[:, :]))
                        elif g == KT - 1:
                            t2 = OP("tensor", "cP", mk)
                        else:
                            if g == 4:
                                W("tensor", "cV", cp1)
                            OPn("tensor", mk)
                    cxc["ctm_free"] = cnt["cP"]
                    W("vector", "cP", t2)
                    dst2 = ap_nd(cfm[0:128, 0:1], 4 * 1024 + s8 * 128,
                                 [(8192, 128), (1024, 4), (1, 128)])
                    cfm_cv = OP("vector", "cV", lambda eng, d=dst2, pT=pT: eng.tensor_copy(out=d, in_=pT[:, :]))
                    W("tensor", "cV", cfm_cv)
                W("tensor", "cV", cfm_cv)
                W("tensor", "cA", wo_ready)
                W("scalar", "dS", bold)
                for dm in range(KT):
                    for half in range(2):
                        pa_ = cxc["pA"] if half == 0 else cxc["pC"]
                        t0 = half * 512
                        amm = 0
                        for k in range(KT):
                            mk = (lambda eng, k=k, dm=dm, pa_=pa_, t0=t0: eng.matmul(
                                pa_[:, :], wo_sb[:, k * 1024 + dm * 128:k * 1024 + dm * 128 + 128],
                                cfm[:, k * 1024 + t0:k * 1024 + t0 + 512],
                                start=(k == 0), stop=(k == KT - 1)))
                            if k == KT - 1:
                                amm = OP("tensor", "cP", mk)
                            else:
                                OPn("tensor", mk)
                        W("scalar", "cP", amm)
                        aev = OP("scalar", "cA", lambda eng, dm=dm, pa_=pa_: eng.activation(
                            out=asb[:, :], in_=pa_[:, :], func=AF.Copy))
                        W("vector", "cA", aev)
                        W("vector", "cV", bo_cv)
                        x2v = OP("vector", "cV", lambda eng, dm=dm, t0=t0: eng.tensor_tensor(
                            out=cxc["xf"][:, dm * 1024 + t0:dm * 1024 + t0 + 512],
                            in0=cxc["xf"][:, dm * 1024 + t0:dm * 1024 + t0 + 512],
                            in1=asb[:, :], op=ALU.add))
                        W("tensor", "cV", x2v)
                        W("scalar", "cV", x2v)
                wo_pe = cnt["cP"]
                cfm_free_pe = wo_pe
                b0 = ln_pass(cxc, 0, src_cv=cnt["cV"])
                b1 = ln_pass(cxc, 1, src_cv=cnt["cV"])
                c0 = ln_norm(cxc, b0[0], b1[0], pe_clear=lnb_free_pe)
                W("tensor", "cV", c0[0])
                W("tensor", "cG", c0[1])
                lastcv, _, down_pe = ffn_block(cxc, None, 0.5, stash_x1=None,
                                               xf_read=(c0[0], c0[1]))
                lnb_free_pe = down_pe
                for dm in range(KT):
                    W("gpsimd", "cV", cnt["cV"])
                    out_dma = DMA("gpsimd", out[dm * 128:(dm + 1) * 128, m * 1024:(m + 1) * 1024],
                                  cxc["xf"][:, dm * 1024:(dm + 1) * 1024], "dG")
                xf_free = (cnt["cV"], cnt["cG"], out_dma, cnt["cA"])

            W("gpsimd", "dG", cnt["dG"])

        with nc.Block() as block:
            @block.sync
            def _(eng):
                for fn in sched["sync"]:
                    fn(eng)

            @block.gpsimd
            def _(eng):
                for fn in sched["gpsimd"]:
                    fn(eng)

            @block.scalar
            def _(eng):
                for fn in sched["scalar"]:
                    fn(eng)

            @block.vector
            def _(eng):
                for fn in sched["vector"]:
                    fn(eng)

            @block.tensor
            def _(eng):
                for fn in sched["tensor"]:
                    fn(eng)

    return nc


_NC_CACHE = {}


def kernel(L0_attn_x, L0_attn_W_A, L0_attn_W_V, L0_attn_W_O, L0_attn_b_O,
           L0_ffn_w_gate, L0_ffn_w_up, L0_ffn_w_down):
    x = np.ascontiguousarray(np.asarray(L0_attn_x, dtype=np.float32))
    W_A = np.asarray(L0_attn_W_A, dtype=np.float32)
    W_V = np.asarray(L0_attn_W_V, dtype=np.float32)
    W_O = np.asarray(L0_attn_W_O, dtype=np.float32)
    b_O = np.asarray(L0_attn_b_O, dtype=np.float32)
    wgi = np.asarray(L0_ffn_w_gate, dtype=np.int32)
    wui = np.asarray(L0_ffn_w_up, dtype=np.int32)
    wdi = np.asarray(L0_ffn_w_down, dtype=np.int32)

    def tile_fd(w, FT_, KT_):
        return np.ascontiguousarray(
            w.reshape(FT_, 128, KT_, 128).transpose(0, 3, 2, 1).reshape(FT_ * 128, KT_ * 128))

    wg_s = tile_fd(wgi, FT, KT)
    wu_s = tile_fd(wui, FT, KT)
    wd_s = tile_fd(wdi, KT, FT)
    wv_s = tile_fd(W_V, KT, KT).astype(np.float32)
    wo_s = tile_fd(W_O, KT, KT).astype(np.float32)
    wa_s = np.ascontiguousarray(W_A.T.astype(np.float32))
    bo_s = np.ascontiguousarray(b_O.reshape(KT, 128).T)
    ones_s = np.ones((128, 1), dtype=ml_dtypes.bfloat16)
    ident_s = np.eye(128, dtype=ml_dtypes.bfloat16)

    if "nc" not in _NC_CACHE:
        _NC_CACHE["nc"] = build()
    nc = _NC_CACHE["nc"]

    in_maps = []
    for c in range(8):
        b = c // 2
        in_maps.append(dict(
            xt=np.ascontiguousarray(x[b].T),
            wg=wg_s, wu=wu_s, wd=wd_s, wv=wv_s, wo=wo_s, wa=wa_s, bo=bo_s,
            ones=ones_s, ident=ident_s,
            cstin=np.tile(np.array([[EPS, 0.125]], np.float32), (128, 1)),
        ))
    res = run_bass_kernel_spmd(nc, in_maps, core_ids=list(range(8)))
    outp = np.empty((B, S, D), dtype=np.float32)
    for c in range(8):
        b, r = c // 2, c % 2
        outp[b, r * (S // 2):(r + 1) * (S // 2), :] = np.asarray(res.results[c]["out"]).T
    return outp

